# version 18
# baseline (speedup 1.0000x reference)
import contextlib
import numpy as np
import ml_dtypes
import concourse.bass as bass
import concourse.mybir as mybir
from concourse.bass_utils import run_bass_kernel_spmd

F32 = mybir.dt.float32
BF16 = mybir.dt.bfloat16
I32 = mybir.dt.int32
AF = mybir.ActivationFunctionType
ALU = mybir.AluOpType
AX = mybir.AxisListType
RG = [[0, 1, 2, 3, 4, 5, 6, 7]]

QMAX = 32767.0
WQMAX = 7.0
NT = 16
D = 1024
I3 = 3072
EPS = 1e-5
SSCALE = 0.03125


class Bld:
    def __init__(self, nc):
        self.nc = nc
        self.prog = {e: [] for e in ("sync", "tensor", "vector", "scalar", "gpsimd")}
        self.cnt = {}
        self.sems = {}

    def sem(self, name, es):
        self.sems[name] = es.enter_context(self.nc.semaphore(name))
        self.cnt[name] = 0

    def emit(self, eng, fn):
        self.prog[eng].append(fn)

    def wait(self, eng, sem, val):
        if val is None or val <= 0:
            return
        h = self.sems[sem]
        self.emit(eng, lambda e, h=h, v=val: e.wait_ge(h, v))

    def barrier(self, eng, sems):
        for s in sems:
            self.wait(eng, s, self.cnt[s])

    def dma(self, eng, out, in_, sem):
        h = self.sems[sem]
        self.cnt[sem] += 16
        self.emit(eng, lambda e, o=out, i=in_, h=h: e.dma_start(out=o, in_=i).then_inc(h, 16))
        return self.cnt[sem]

    def op(self, eng, fn, sem=None):
        if sem is None:
            self.emit(eng, fn)
            return None
        h = self.sems[sem]
        self.cnt[sem] += 1
        self.emit(eng, lambda e, f=fn, h=h: f(e).then_inc(h, 1))
        return self.cnt[sem]


class Arena:
    def __init__(self, nc, name, n, dtype, es):
        self.t = es.enter_context(nc.sbuf_tensor(name, [128, n], dtype))
        self.n = n
        self.off = 0
        self.marks = []

    def alloc(self, shape):
        sz = int(np.prod(shape))
        assert self.off + sz <= self.n, f"arena {self.t.name} overflow {self.off}+{sz}>{self.n}"
        ap = self.t[:, self.off:self.off + sz]
        if len(shape) > 1:
            letters = "abcdefg"[:len(shape)]
            pat = "p (" + " ".join(letters) + ") -> p " + " ".join(letters)
            ap = ap.rearrange(pat, **{letters[i]: shape[i] for i in range(len(shape) - 1)})
        self.off += sz
        return ap

    def push(self):
        self.marks.append(self.off)

    def pop(self):
        self.off = self.marks.pop()


def build():
    nc = bass.Bass("TRN2", target_bir_lowering=False)

    x_in = nc.dram_tensor("x_own", [NT, 128, D], F32, kind="ExternalInput").ap()
    W_in = nc.dram_tensor("W", [D, 3 * D], F32, kind="ExternalInput").ap()
    bq_in = nc.dram_tensor("b_q", [128, 8], F32, kind="ExternalInput").ap()
    bk_in = nc.dram_tensor("b_k", [128, 8], F32, kind="ExternalInput").ap()
    bv_in = nc.dram_tensor("b_v", [1, D], F32, kind="ExternalInput").ap()
    wg_in = nc.dram_tensor("wg_sl", [128, 3, D], F32, kind="ExternalInput").ap()
    wu_in = nc.dram_tensor("wu_sl", [128, 3, D], F32, kind="ExternalInput").ap()
    wd_in = nc.dram_tensor("wd_sl", [128, I3], F32, kind="ExternalInput").ap()
    mk_in = nc.dram_tensor("mask1", [128, 1024], F32, kind="ExternalInput").ap()
    mkn_in = nc.dram_tensor("maskneg", [128, 1024], F32, kind="ExternalInput").ap()
    idb_in = nc.dram_tensor("ident_bf", [128, 128], BF16, kind="ExternalInput").ap()
    idf_in = nc.dram_tensor("ident_f", [128, 128], F32, kind="ExternalInput").ap()
    ones_in = nc.dram_tensor("ones_f", [1, 128], F32, kind="ExternalInput").ap()
    out_d = nc.dram_tensor("out_own", [NT, 128, D], F32, kind="ExternalOutput").ap()


    kt_own2 = [nc.dram_tensor(f"kt_own{bi}", [8, 128, 8, 128], BF16, kind="Internal").ap()
               for bi in range(2)]
    kt_g2 = [nc.dram_tensor(f"kt_g{bi}", [8, 8, 128, 8, 128], BF16, kind="Internal",
                            addr_space="Shared").ap()
             for bi in range(2)]
    v_own2 = [nc.dram_tensor(f"v_own{bi}", [8, 128, D], BF16, kind="Internal").ap()
              for bi in range(2)]
    v_g2 = [nc.dram_tensor(f"v_g{bi}", [8, 8, 128, D], BF16, kind="Internal",
                           addr_space="Shared").ap()
            for bi in range(2)]
    wq_own = nc.dram_tensor("wq_own", [3, 8, 128, 384], BF16, kind="Internal").ap()
    wq_g = nc.dram_tensor("wq_g", [8, 3, 8, 128, 384], BF16, kind="Internal",
                          addr_space="Shared").ap()
    xlnT_d = nc.dram_tensor("xlnT_d", [8, 128, NT * 128], BF16, kind="Internal").ap()
    x1_d = nc.dram_tensor("x1_d", [NT, 128, D], F32, kind="Internal").ap()
    gate_d = nc.dram_tensor("gate_d", [24, 128, NT * 128], BF16, kind="Internal").ap()
    up_d = nc.dram_tensor("up_d", [24, 128, NT * 128], BF16, kind="Internal").ap()
    cc_in = [nc.dram_tensor(f"cc_in{i}", [1, 64], F32, kind="Internal").ap() for i in range(4)]
    cc_out = [nc.dram_tensor(f"cc_out{i}", [1, 64], F32, kind="Internal",
                             addr_space="Shared").ap() for i in range(4)]

    es = contextlib.ExitStack()
    with es:
        af = Arena(nc, "af", 17400, F32, es)
        ab = Arena(nc, "ab", 60544, BF16, es)
        ai = Arena(nc, "ai", 4096, I32, es)

        masks = af.alloc((1024,))
        masksn = af.alloc((1024,))
        bvb = af.alloc((D,))
        bq = af.alloc((8,))
        bk = af.alloc((8,))
        identf = af.alloc((128,))
        onesf = af.alloc((128,))
        maxcols = af.alloc((NT,))
        ccs = af.alloc((64,))
        ccr = af.alloc((64,))
        svec = [af.alloc((1,)) for _ in range(5)]
        rvec = [af.alloc((1,)) for _ in range(5)]
        ln = {k: af.alloc((1,)) for k in
              ("sum", "ssq", "mean", "msq", "var", "std", "rs", "nms")}
        sqd = af.alloc((1024,))
        identb = ab.alloc((128,))

        psA = es.enter_context(nc.psum_tensor("psA", [128, 1024], F32))
        psB = es.enter_context(nc.psum_tensor("psB", [128, 1024], F32))
        psV = es.enter_context(nc.psum_tensor("psV", [128, 1024], F32))
        psT = es.enter_context(nc.psum_tensor("psT", [128, 1024], BF16))
        psM = es.enter_context(nc.psum_tensor("psM", [128, 512], F32))

        b = Bld(nc)
        for s in ("qS", "qG", "sT", "sV", "sS", "sGp", "qKV0", "qKV1", "qW", "sQ",
                  "qC", "sQg"):
            b.sem(s, es)

        def hand():
            hq = b.sems["qS"]
            vq = b.cnt["qS"]
            b.emit("sync", lambda e, h=hq, v=vq: e.wait_ge(h, v))
            hs = b.sems["sQ"]
            b.cnt["sQ"] += 1
            b.emit("sync", lambda e, h=hs: e.nop().then_inc(h, 1))
            return b.cnt["sQ"]

        def ghand():
            hq = b.sems["qG"]
            vq = b.cnt["qG"]
            b.emit("gpsimd", lambda e, h=hq, v=vq: e.wait_ge(h, v))
            hs = b.sems["sQg"]
            b.cnt["sQg"] += 1
            b.emit("gpsimd", lambda e, h=hs: e.nop().then_inc(h, 1))
            return b.cnt["sQg"]

        def ln_apply(tile_f32, out_ap):
            b.barrier("scalar", ["sV", "sS"])
            b.barrier("vector", ["sS", "sV"])
            vs = b.op("scalar", lambda e: e.activation(
                out=sqd, in_=tile_f32, func=AF.Square, accum_out=ln["ssq"]), "sS")
            vv = b.op("vector", lambda e: e.tensor_reduce(
                out=ln["sum"], in_=tile_f32, axis=AX.X, op=ALU.add), "sV")
            b.wait("vector", "sS", vs)
            b.wait("vector", "sV", vv)
            b.op("vector", lambda e: e.tensor_scalar(
                out=ln["mean"], in0=ln["sum"], scalar1=1.0 / D, scalar2=None, op0=ALU.mult))
            vv = b.op("vector", lambda e: e.tensor_scalar(
                out=ln["msq"], in0=ln["ssq"], scalar1=1.0 / D, scalar2=None, op0=ALU.mult), "sV")
            b.wait("vector", "sV", vv)
            vv = b.op("vector", lambda e: e.tensor_scalar(
                out=ln["var"], in0=ln["mean"], scalar1=ln["mean"], scalar2=None, op0=ALU.mult), "sV")
            b.wait("vector", "sV", vv)
            vv = b.op("vector", lambda e: e.scalar_tensor_tensor(
                out=ln["var"], in0=ln["msq"], scalar=EPS, in1=ln["var"],
                op0=ALU.add, op1=ALU.subtract), "sV")
            b.wait("vector", "sV", vv)
            b.wait("scalar", "sV", vv)
            vs = b.op("scalar", lambda e: e.activation(
                out=ln["std"], in_=ln["var"], func=AF.Sqrt), "sS")
            b.wait("vector", "sS", vs)
            vv = b.op("vector", lambda e: e.reciprocal(out=ln["rs"], in_=ln["std"]), "sV")
            b.wait("vector", "sV", vv)
            vv = b.op("vector", lambda e: e.tensor_scalar(
                out=ln["nms"], in0=ln["mean"], scalar1=ln["rs"], scalar2=-1.0,
                op0=ALU.mult, op1=ALU.mult), "sV")
            b.wait("scalar", "sV", vv)
            return b.op("scalar", lambda e, o=out_ap, t=tile_f32: e.activation(
                out=o, in_=t, func=AF.Identity, scale=ln["rs"], bias=ln["nms"]), "sS")

        def fold_max(src_col, col):
            b.barrier("tensor", ["sV", "sGp"])
            vt = b.op("tensor", lambda e, s=src_col: e.matmul(
                out=psM[0:1, 0:128], lhsT=s, rhs=identf, start=True, stop=True), "sT")
            b.wait("vector", "sT", vt)
            vv = b.op("vector", lambda e, c=col: e.tensor_reduce(
                out=ccs[0:1, c:c + 1], in_=psM[0:1, 0:128], axis=AX.X,
                op=ALU.max, apply_absolute_value=True), "sV")
            b.wait("tensor", "sV", vv)
            return vv

        def allreduce_scales(round_i, cols):
            b.barrier("gpsimd", ["sV"])
            wv = b.dma("gpsimd", cc_in[round_i], ccs[0:1, :], "qG")
            b.wait("gpsimd", "qG", wv)
            cv = b.op("gpsimd", lambda e, i=round_i: e.collective_compute(
                "AllReduce", ALU.max, RG, ins=[cc_in[i].opt()], outs=[cc_out[i].opt()]), "qC")
            b.wait("gpsimd", "qC", cv)
            rv = b.dma("gpsimd", ccr[0:1, :], cc_out[round_i], "qG")
            b.wait("tensor", "qG", rv)
            for col, idx in cols:
                vt = b.op("tensor", lambda e, c=col: e.matmul(
                    out=psM[:, 0:1], lhsT=onesf[0:1, :], rhs=ccr[0:1, c:c + 1],
                    start=True, stop=True), "sT")
                b.wait("vector", "sT", vt)
                vv = b.op("vector", lambda e, i=idx: e.tensor_scalar(
                    out=svec[i], in0=psM[:, 0:1], scalar1=1.0 / QMAX, scalar2=1e-12,
                    op0=ALU.mult, op1=ALU.max), "sV")
                b.wait("vector", "sV", vv)
                vv = b.op("vector", lambda e, i=idx: e.reciprocal(
                    out=rvec[i], in_=svec[i]), "sV")
                b.wait("tensor", "sV", vv)
            vv = b.cnt["sV"]
            for eng in ("vector", "gpsimd", "scalar"):
                b.wait(eng, "sV", vv)
            return vv

        b.dma("sync", masks, mk_in, "qS")
        b.dma("sync", masksn, mkn_in, "qS")
        b.dma("sync", bq, bq_in, "qS")
        b.dma("sync", bk, bk_in, "qS")
        b.dma("sync", identf, idf_in, "qS")
        b.dma("sync", onesf[0:1, :], ones_in, "qS")
        b.dma("sync", identb, idb_in, "qS")
        b.op("vector", lambda e: e.memset(ccs[0:1, :], 0.0), "sV")

        af.push()
        bvrow = af.alloc((D,))

        b.dma("sync", bvrow[0:1, :], bv_in, "qS")
        cv0 = hand()
        b.wait("tensor", "sQ", cv0)
        for h in range(2):
            vt = b.op("tensor", lambda e, h=h: e.matmul(
                out=psM[:, 0:512], lhsT=onesf[0:1, :], rhs=bvrow[0:1, 512 * h:512 * (h + 1)],
                start=True, stop=True), "sT")
            b.wait("vector", "sT", vt)
            vv = b.op("vector", lambda e, h=h: e.tensor_copy(
                out=bvb[:, 512 * h:512 * (h + 1)], in_=psM[:, 0:512]), "sV")
            b.wait("tensor", "sV", vv)
        af.pop()

        af.push(); ab.push(); ai.push()
        qT = ab.alloc((8, 2048))

        af.push(); ab.push()
        b.barrier("sync", ["sT", "sV", "sS"])
        W_sb = ab.alloc((8, 3 * D))
        xlnT = ab.alloc((8, 2048))
        xlnb = ab.alloc((D,))
        kvst = ab.alloc((4, 512))
        Wstg = af.alloc((2, 3 * D))
        xst = af.alloc((2, D))

        last_cast = [0, 0]
        for c in range(8):
            buf = c % 2
            if buf == 0:
                b.wait("sync", "sV", last_cast[buf])
                b.dma("sync", Wstg[:, buf, :], W_in[c * 128:(c + 1) * 128, :], "qS")
                tick, tsem = hand(), "sQ"
            else:
                b.wait("gpsimd", "sV", last_cast[buf])
                b.dma("gpsimd", Wstg[:, buf, :], W_in[c * 128:(c + 1) * 128, :], "qG")
                tick, tsem = ghand(), "sQg"
            b.wait("vector", tsem, tick)
            last_cast[buf] = b.op("vector", lambda e, c=c, buf=buf: e.tensor_copy(
                out=W_sb[:, c, :], in_=Wstg[:, buf, :]), "sV")

        xst_free = [0, 0]
        for k in range(NT):
            buf = k % 2
            b.wait("sync", "sS", xst_free[buf])
            b.dma("sync", xst[:, buf, :], x_in[k], "qS")
            lv = hand()
            b.wait("scalar", "sQ", lv)
            b.wait("vector", "sQ", lv)
            b.barrier("scalar", ["sT"])
            sv = ln_apply(xst[:, buf, :], xlnb)
            xst_free[buf] = sv
            b.wait("tensor", "sS", sv)
            b.barrier("tensor", ["sV"])
            for c in range(8):
                vt = b.op("tensor", lambda e, c=c: e.transpose(
                    out=psT[:, c * 128:(c + 1) * 128],
                    in_=xlnb[:, c * 128:(c + 1) * 128], identity=identb),
                    "sT" if c == 7 else None)
            b.wait("vector", "sT", vt)
            vv = b.op("vector", lambda e, k=k: e.tensor_copy(
                out=xlnT[:, :, k * 128:(k + 1) * 128],
                in_=psT[:].rearrange("p (c t) -> p c t", c=8)), "sV")
            b.wait("tensor", "sV", vv)

        pslot = [psA[:, 0:512], psA[:, 512:1024], psB[:, 0:512], psB[:, 512:1024]]
        pfree = [0, 0, 0, 0]
        pfeng = [None] * 4
        si = 0
        kst_hand = [0, 0, 0, 0]
        gather_done = {}
        hv = hand()
        b.wait("tensor", "sQ", hv)
        b.barrier("tensor", ["sV"])
        for th in range(4):
            for j in range(16):
                sl = si % 4
                if pfeng[sl] is not None:
                    b.wait("tensor", pfeng[sl], pfree[sl])
                for c in range(8):
                    mv = b.op("tensor", lambda e, j=j, th=th, c=c, sl=sl: e.matmul(
                        out=pslot[sl], lhsT=W_sb[:, c, j * 128:(j + 1) * 128],
                        rhs=xlnT[:, c, th * 512:(th + 1) * 512],
                        start=(c == 0), stop=(c == 7)), "sT" if c == 7 else None)
                if j < 8:
                    b.wait("scalar", "sT", mv)
                    ev = b.op("scalar", lambda e, j=j, th=th, sl=sl: e.activation(
                        out=qT[:, j, th * 512:(th + 1) * 512], in_=pslot[sl],
                        func=AF.Identity, bias=bq[:, j:j + 1]), "sS")
                    pfree[sl], pfeng[sl] = ev, "sS"
                else:
                    jc = j - 8
                    kb = si % 4
                    sbuf = kvst[:, kb, :]
                    b.wait("scalar", "sT", mv)
                    b.wait("scalar", "sQ", kst_hand[kb])
                    ev = b.op("scalar", lambda e, jc=jc, sl=sl, sbuf=sbuf: e.activation(
                        out=sbuf, in_=pslot[sl], func=AF.Identity,
                        bias=bk[:, jc:jc + 1]), "sS")
                    pfree[sl], pfeng[sl] = ev, "sS"
                    b.wait("sync", "sS", ev)
                    bi, t4 = th // 2, th % 2
                    b.dma(
                        "sync",
                        kt_own2[bi][t4 * 4:(t4 + 1) * 4, :, jc, :].rearrange(
                            "k p t -> p k t"),
                        sbuf.rearrange("p (k t) -> p k t", k=4), "qS")
                    if kb == 3:
                        hv = hand()
                        kst_hand[0] = kst_hand[1] = kst_hand[2] = kst_hand[3] = hv
                si += 1
            for kt_i in range(th * 4, (th + 1) * 4):
                for h in range(2):
                    sl = si % 4
                    if pfeng[sl] is not None:
                        b.wait("tensor", pfeng[sl], pfree[sl])
                    for c in range(8):
                        mv = b.op("tensor", lambda e, kt_i=kt_i, h=h, c=c, sl=sl: e.matmul(
                            out=pslot[sl],
                            lhsT=xlnT[:, c, kt_i * 128:(kt_i + 1) * 128],
                            rhs=W_sb[:, c, 2048 + h * 512:2048 + (h + 1) * 512],
                            start=(c == 0), stop=(c == 7)), "sT" if c == 7 else None)
                    kb = si % 4
                    sbuf = kvst[:, kb, :]
                    b.wait("vector", "sT", mv)
                    b.wait("vector", "sQ", kst_hand[kb])
                    ev = b.op("vector", lambda e, h=h, sl=sl, sbuf=sbuf: e.tensor_tensor(
                        out=sbuf, in0=pslot[sl], in1=bvb[:, h * 512:(h + 1) * 512],
                        op=ALU.add), "sV")
                    pfree[sl], pfeng[sl] = ev, "sV"
                    b.wait("sync", "sV", ev)
                    bi = kt_i // 8
                    b.dma("sync", v_own2[bi][kt_i % 8, :, h * 512:(h + 1) * 512],
                          sbuf, "qS")
                    if kb == 3:
                        hv = hand()
                        kst_hand[0] = kst_hand[1] = kst_hand[2] = kst_hand[3] = hv
                    si += 1
            if th == 1 or th == 3:
                bi = th // 2
                hv = hand()
                b.wait("gpsimd", "sQ", hv)
                b.op("gpsimd", lambda e, bi=bi: e.collective_compute(
                    "AllGather", ALU.bypass, RG,
                    ins=[kt_own2[bi].opt()], outs=[kt_g2[bi].opt()]), f"qKV{bi}")
                gather_done[bi] = b.op(
                    "gpsimd", lambda e, bi=bi: e.collective_compute(
                        "AllGather", ALU.bypass, RG,
                        ins=[v_own2[bi].opt()], outs=[v_g2[bi].opt()]), f"qKV{bi}")
        af.pop(); ab.pop()

        af.push(); ab.push(); ai.push()
        wsl = af.alloc((3 * D,))
        wabs = af.alloc((3 * D,))
        bmt = af.alloc((3, 1536))
        wq_bf = ab.alloc((3 * D,))
        wtr = ab.alloc((1024,))
        qi32 = ai.alloc((3 * D,))
        b.barrier("sync", ["sT", "sS", "sV"])

        def wqdq(n_free):
            nb = n_free // 32
            b.barrier("vector", ["sT"])
            vv = b.op("vector", lambda e, n=n_free: e.scalar_tensor_tensor(
                out=wabs[:, 0:n], in0=wsl[:, 0:n], scalar=-1.0, in1=wsl[:, 0:n],
                op0=ALU.mult, op1=ALU.max), "sV")
            b.wait("vector", "sV", vv)
            v3 = wabs[:, 0:n_free].rearrange("p (a two c) -> p a two c", a=nb, two=2)
            cur = bmt[:, 0, 0:nb * 16].rearrange("p (a c) -> p a c", a=nb)
            vv = b.op("vector", lambda e, cur=cur, v3=v3: e.tensor_tensor(
                out=cur, in0=v3[:, :, 0, :], in1=v3[:, :, 1, :], op=ALU.max), "sV")
            b.wait("vector", "sV", vv)
            w, buf = 16, 1
            while w > 1:
                nxt = bmt[:, buf, 0:nb * (w // 2)].rearrange("p (a c) -> p a c", a=nb)
                vv = b.op("vector", lambda e, cur=cur, nxt=nxt: e.tensor_tensor(
                    out=nxt, in0=cur[:, :, 0::2], in1=cur[:, :, 1::2], op=ALU.max), "sV")
                b.wait("vector", "sV", vv)
                cur, w, buf = nxt, w // 2, buf ^ 1
            sb_ = bmt[:, 2, 0:nb]
            rb_ = bmt[:, 2, 1536 - nb:1536]
            vv = b.op("vector", lambda e, cur=cur, sb_=sb_: e.tensor_scalar(
                out=sb_, in0=cur.rearrange("p a c -> p (a c)"),
                scalar1=1.0 / WQMAX, scalar2=1e-12, op0=ALU.mult, op1=ALU.max), "sV")
            b.wait("vector", "sV", vv)
            vv = b.op("vector", lambda e, sb_=sb_, rb_=rb_: e.reciprocal(
                out=rb_, in_=sb_), "sV")
            b.wait("vector", "sV", vv)
            vv = b.op("vector", lambda e, nb=nb, rb_=rb_: e.tensor_tensor(
                out=qi32[:, 0:n_free].rearrange("p (a c) -> p a c", a=nb),
                in0=wsl[:, 0:n_free].rearrange("p (a c) -> p a c", a=nb),
                in1=rb_.rearrange("p (a c) -> p a c", a=nb).broadcast_to((128, nb, 32)),
                op=ALU.mult), "sV")
            b.wait("vector", "sV", vv)
            return b.op("vector", lambda e, nb=nb, sb_=sb_: e.tensor_tensor(
                out=wq_bf[:, 0:n_free].rearrange("p (a c) -> p a c", a=nb),
                in0=qi32[:, 0:n_free].rearrange("p (a c) -> p a c", a=nb),
                in1=sb_.rearrange("p (a c) -> p a c", a=nb).broadcast_to((128, nb, 32)),
                op=ALU.mult), "sV")

        last_store = 0
        for widx, src in ((0, wg_in), (1, wu_in)):
            b.barrier("sync", ["sV"])
            b.dma("sync", wsl.rearrange("p (a c) -> p a c", a=3), src, "qS")
            lv = hand()
            b.wait("vector", "sQ", lv)
            qv = wqdq(3 * D)
            b.wait("tensor", "sV", qv)
            for c in range(8):
                for o3 in range(3):
                    vt = b.op("tensor", lambda e, c=c, o3=o3: e.transpose(
                        out=psT[:, o3 * 128:(o3 + 1) * 128],
                        in_=wq_bf[:, (o3 * D + c * 128):(o3 * D + (c + 1) * 128)],
                        identity=identb), "sT" if o3 == 2 else None)
                b.wait("vector", "sT", vt)
                b.wait("vector", "sQ", last_store)
                cvv = b.op("vector", lambda e: e.tensor_copy(
                    out=wtr[:, 0:384], in_=psT[:, 0:384]), "sV")
                b.wait("sync", "sV", cvv)
                b.dma("sync", wq_own[widx, c, :, :], wtr[:, 0:384], "qS")
                last_store = hand()
                b.wait("tensor", "sV", cvv)
        b.barrier("sync", ["sV"])
        b.dma("sync", wsl, wd_in, "qS")
        lv = hand()
        b.wait("vector", "sQ", lv)
        qv = wqdq(I3)
        b.wait("tensor", "sV", qv)
        wd_view = wq_own[2].rearrange("c p o -> (c p o)").rearrange(
            "(i p o) -> i p o", i=24, p=128)
        for i8 in range(3):
            for j in range(8):
                i = i8 * 8 + j
                vt = b.op("tensor", lambda e, i=i, j=j: e.transpose(
                    out=psT[:, j * 128:(j + 1) * 128],
                    in_=wq_bf[:, i * 128:(i + 1) * 128], identity=identb),
                    "sT" if j == 7 else None)
            b.wait("vector", "sT", vt)
            b.wait("vector", "sQ", last_store)
            cvv = b.op("vector", lambda e: e.tensor_copy(out=wtr, in_=psT[:]), "sV")
            b.wait("sync", "sV", cvv)
            b.dma("sync",
                  wd_view[i8 * 8:(i8 + 1) * 8, :, :].rearrange("i p o -> p i o"),
                  wtr.rearrange("p (i o) -> p i o", i=8), "qS")
            last_store = hand()
            b.wait("tensor", "sV", cvv)
        hv = hand()
        b.wait("gpsimd", "sQ", hv)
        wq_gather_done = b.op("gpsimd", lambda e: e.collective_compute(
            "AllGather", ALU.bypass, RG, ins=[wq_own.opt()], outs=[wq_g.opt()]), "qW")
        af.pop(); ab.pop(); ai.pop()

        acc = af.alloc((8, D))
        mrun = af.alloc((8,))
        lrun = af.alloc((8,))
        mnt = af.alloc((1,))
        mneg = af.alloc((1,))
        alpha = af.alloc((1,))
        cmax = af.alloc((1,))
        rowsum = af.alloc((1,))
        recl = af.alloc((1,))
        xst2 = af.alloc((D,))
        aout = af.alloc((D,))
        x1t = af.alloc((D,))
        ktc = ab.alloc((2, 8, 8, 128))
        vc = ab.alloc((2, 8, 1024))
        expb = ab.alloc((1024,))
        attnT = ab.alloc((1024,))
        xlnb2 = ab.alloc((D,))
        xlts2 = ab.alloc((8, 128))

        b.barrier("tensor", ["sS", "sV"])
        b.barrier("sync", ["sT", "sV", "sS"])

        xst2_free = 0
        x1t_store = 0
        xlts2_store = 0
        scoreps = [psA, psB]
        score_free = [0, 0]
        ktc_free = [0, 0]
        vc_free = [0, 0]
        ktc_ready = [0, 0]
        vc_ready = [0, 0]
        last_exp_consumed = 0
        last_attnT_used = 0
        last_acc_upd = 0

        for bat in range(2):
            b.wait("sync", f"qKV{bat}", gather_done[bat])
            visits = [(c, m) for c in range(8) for m in range(c, 8)]
            loads_done = {}
            mvs = {}

            def emit_loads(c):
                cb = c % 2
                b.wait("sync", "sT", ktc_free[cb])
                for r in range(8):
                    b.dma("sync", ktc[:, cb, r, :, :].rearrange("p c t -> p (c t)"),
                          kt_g2[bat][r, c].rearrange("p c t -> p (c t)"), "qS")
                b.wait("sync", "sT", vc_free[cb])
                for r in range(8):
                    b.dma("sync", vc[:, cb, r, :], v_g2[bat][r, c], "qS")
                wv = hand()
                ktc_ready[cb] = wv
                vc_ready[cb] = wv
                loads_done[c] = True

            def emit_A(j):
                c, m = visits[j]
                cb = c % 2
                if c not in loads_done:
                    emit_loads(c)
                sp = scoreps[m % 2]
                k_tile = bat * 8 + m
                b.wait("tensor", "sQ", ktc_ready[cb])
                b.wait("tensor", "sS", score_free[m % 2])
                for h in range(2):
                    for cc in range(8):
                        mv = b.op("tensor", lambda e, h=h, cc=cc, sp=sp, cb=cb, k_tile=k_tile: e.matmul(
                            out=sp[:, h * 512:(h + 1) * 512],
                            lhsT=qT[:, cc, k_tile * 128:(k_tile + 1) * 128],
                            rhs=ktc[:, cb, h * 4:(h + 1) * 4, cc, :],
                            start=(cc == 0), stop=(cc == 7)),
                            "sT" if (cc == 7 and h == 1) else None)
                mvs[j] = mv
                if m == 7:
                    ktc_free[cb] = mv

            def emit_B(j):
                nonlocal last_exp_consumed, last_attnT_used, last_acc_upd
                c, m = visits[j]
                cb = c % 2
                sp = scoreps[m % 2]
                mv = mvs[j]
                b.wait("vector", "sT", mv)
                if m == c:
                    vv = b.op("vector", lambda e, sp=sp: e.tensor_tensor(
                        out=sp[:], in0=sp[:], in1=masks, op=ALU.mult), "sV")
                    b.wait("vector", "sV", vv)
                    vv = b.op("vector", lambda e, sp=sp: e.tensor_tensor(
                        out=sp[:], in0=sp[:], in1=masksn, op=ALU.add), "sV")
                    b.wait("vector", "sV", vv)
                vv = b.op("vector", lambda e, sp=sp: e.tensor_reduce(
                    out=cmax, in_=sp[:], axis=AX.X, op=ALU.max), "sV")
                b.wait("vector", "sV", vv)
                first = (c == 0)
                if first:
                    b.op("vector", lambda e, m=m: e.tensor_copy(
                        out=mrun[:, m:m + 1], in_=cmax))
                    vv = b.op("vector", lambda e: e.tensor_scalar(
                        out=mneg, in0=cmax, scalar1=-SSCALE, scalar2=None,
                        op0=ALU.mult), "sV")
                    b.wait("scalar", "sV", vv)
                else:
                    vv = b.op("vector", lambda e, m=m: e.tensor_tensor(
                        out=mnt, in0=mrun[:, m:m + 1], in1=cmax, op=ALU.max), "sV")
                    b.wait("vector", "sV", vv)
                    vv = b.op("vector", lambda e: e.tensor_scalar(
                        out=mneg, in0=mnt, scalar1=-SSCALE, scalar2=None,
                        op0=ALU.mult), "sV")
                    b.wait("scalar", "sV", vv)
                    av = b.op("scalar", lambda e, m=m: e.activation(
                        out=alpha, in_=mrun[:, m:m + 1], func=AF.Exp,
                        scale=SSCALE, bias=mneg), "sS")
                    b.wait("vector", "sS", av)
                    b.op("vector", lambda e, m=m: e.tensor_copy(
                        out=mrun[:, m:m + 1], in_=mnt))
                b.wait("scalar", "sT", last_exp_consumed)
                sv = b.op("scalar", lambda e, sp=sp: e.activation(
                    out=expb, in_=sp[:], func=AF.Exp, scale=SSCALE, bias=mneg,
                    accum_out=rowsum), "sS")
                score_free[m % 2] = sv
                b.wait("tensor", "sS", sv)
                b.barrier("tensor", ["sV"])
                for r in range(8):
                    tv = b.op("tensor", lambda e, r=r: e.transpose(
                        out=psT[:, r * 128:(r + 1) * 128],
                        in_=expb[:, r * 128:(r + 1) * 128], identity=identb),
                        "sT" if r == 7 else None)
                last_exp_consumed = tv
                b.wait("vector", "sT", tv)
                b.wait("vector", "sT", last_attnT_used)
                cvv = b.op("vector", lambda e: e.tensor_copy(out=attnT, in_=psT[:]), "sV")
                b.wait("tensor", "sV", cvv)
                b.wait("tensor", "sQ", vc_ready[cb])
                b.wait("tensor", "sV", last_acc_upd)
                for r in range(8):
                    for h in range(2):
                        pv = b.op("tensor", lambda e, r=r, h=h, cb=cb: e.matmul(
                            out=psV[:, h * 512:(h + 1) * 512],
                            lhsT=attnT[:, r * 128:(r + 1) * 128],
                            rhs=vc[:, cb, r, h * 512:(h + 1) * 512],
                            start=(r == 0), stop=(r == 7)),
                            "sT" if (r == 7 and h == 1) else None)
                last_attnT_used = pv
                if m == 7:
                    vc_free[cb] = pv
                b.wait("vector", "sT", pv)
                b.wait("vector", "sS", sv)
                if first:
                    b.op("vector", lambda e, m=m: e.tensor_copy(
                        out=lrun[:, m:m + 1], in_=rowsum))
                    last_acc_upd = b.op("vector", lambda e, m=m: e.tensor_copy(
                        out=acc[:, m, :], in_=psV[:]), "sV")
                else:
                    b.op("vector", lambda e, m=m: e.tensor_scalar(
                        out=lrun[:, m:m + 1], in0=lrun[:, m:m + 1],
                        scalar1=alpha, scalar2=rowsum,
                        op0=ALU.mult, op1=ALU.add))
                    last_acc_upd = b.op("vector", lambda e, m=m: e.scalar_tensor_tensor(
                        out=acc[:, m, :], in0=acc[:, m, :], scalar=alpha,
                        in1=psV[:], op0=ALU.mult, op1=ALU.add), "sV")

            emit_A(0)
            nv = len(visits)
            for j in range(nv):
                if j + 1 < nv and (visits[j + 1][1] % 2) != (visits[j][1] % 2):
                    emit_A(j + 1)
                    emit_B(j)
                else:
                    emit_B(j)
                    if j + 1 < nv:
                        emit_A(j + 1)

            for m in range(8):
                k_tile = bat * 8 + m
                b.barrier("vector", ["sS"])
                b.barrier("scalar", ["sT"])
                vv = b.op("vector", lambda e, m=m: e.reciprocal(
                    out=recl, in_=lrun[:, m:m + 1]), "sV")
                b.wait("vector", "sV", vv)
                b.wait("scalar", "sV", vv)
                b.wait("sync", "sV", xst2_free)
                b.dma("sync", xst2, x_in[k_tile], "qS")
                lv = hand()
                av = b.op("scalar", lambda e, m=m: e.activation(
                    out=aout, in_=acc[:, m, :], func=AF.Identity, scale=recl), "sS")
                b.wait("vector", "sS", av)
                sv = ln_apply(aout, aout)
                b.wait("vector", "sS", sv)
                b.wait("vector", "sQ", lv)
                b.wait("vector", "sQ", x1t_store)
                vv = b.op("vector", lambda e: e.tensor_tensor(
                    out=x1t, in0=xst2, in1=aout, op=ALU.add), "sV")
                xst2_free = vv
                b.wait("sync", "sV", vv)
                b.dma("sync", x1_d[k_tile], x1t, "qS")
                x1t_store = hand()
                b.wait("scalar", "sV", vv)
                sv = ln_apply(x1t, xlnb2)
                b.wait("vector", "sS", sv)
                b.op("vector", lambda e, k=k_tile: e.tensor_reduce(
                    out=maxcols[:, k:k + 1], in_=xlnb2, axis=AX.X,
                    op=ALU.max, apply_absolute_value=True), "sV")
                b.wait("tensor", "sS", sv)
                b.barrier("tensor", ["sV"])
                for c in range(8):
                    vt = b.op("tensor", lambda e, c=c: e.transpose(
                        out=psT[:, c * 128:(c + 1) * 128],
                        in_=xlnb2[:, c * 128:(c + 1) * 128], identity=identb),
                        "sT" if c == 7 else None)
                b.wait("vector", "sT", vt)
                b.wait("vector", "sQ", xlts2_store)
                cvv = b.op("vector", lambda e: e.tensor_copy(
                    out=xlts2.rearrange("p c t -> p (c t)"), in_=psT[:]), "sV")
                b.wait("sync", "sV", cvv)
                b.dma("sync", xlnT_d[:, :, k_tile * 128:(k_tile + 1) * 128].rearrange(
                    "c p t -> p c t"), xlts2, "qS")
                xlts2_store = hand()
                b.wait("tensor", "sV", cvv)

        b.barrier("vector", ["sV"])
        vv = b.op("vector", lambda e: e.tensor_reduce(
            out=cmax, in_=maxcols, axis=AX.X, op=ALU.max), "sV")
        b.wait("vector", "sV", vv)
        fold_max(cmax, 0)
        allreduce_scales(0, [(0, 0)])

        af.pop(); ab.pop(); ai.pop()

        af.push(); ab.push(); ai.push()
        b.barrier("sync", ["sT", "sV", "sS"])
        gmaxc = af.alloc((96,))
        umaxc = af.alloc((96,))
        gq = af.alloc((2048,))
        sg2 = af.alloc((2048,))
        xffn = af.alloc((D,))
        oout = af.alloc((D,))
        wT = ab.alloc((8, I3))
        xqT = ab.alloc((8, 2048))
        gstg = ab.alloc((4, 512))
        gchunk = ab.alloc((4, 2048))
        ocol = ab.alloc((2, 24, 128))
        i32t = ai.alloc((2048,))
        i32u = ai.alloc((2048,))

        b.dma("sync", xqT, xlnT_d.rearrange("c p t -> p c t"), "qS")
        lv = hand()
        b.wait("vector", "sQ", lv)
        for c in range(8):
            vv = b.op("vector", lambda e, c=c: e.tensor_scalar(
                out=i32t, in0=xqT[:, c, :], scalar1=rvec[0], scalar2=None, op0=ALU.mult), "sV")
            b.wait("vector", "sV", vv)
            vv = b.op("vector", lambda e, c=c: e.tensor_scalar(
                out=xqT[:, c, :], in0=i32t, scalar1=svec[0], scalar2=None, op0=ALU.mult),
                "sV")
            b.wait("vector", "sV", vv)
        xq_ready = b.cnt["sV"]

        b.wait("sync", "qW", wq_gather_done)

        def load_wT(widx):
            for R in range(8):
                for c in range(8):
                    b.dma("sync", wT[:, c, 384 * R:384 * (R + 1)],
                          wq_g[R, widx, c, :, :], "qS")
            return hand()

        sgq = af.alloc((2048,))

        chain1_state = {"p2_last": [0, 0], "p2r_last": [0, 0], "started": False,
                        "gate_store_done": 0, "ld_tick": [0, 0], "s": 0,
                        "i32t_free": 0, "u16_v": 0}

        def c1_load(i):
            st = chain1_state
            cb = i % 2
            b.wait("sync", "sQ", st["p2_last"][cb])
            b.wait("sync", "sV", st["p2r_last"][cb])
            if not st["started"]:
                b.wait("sync", "sQg", st["gate_store_done"])
                st["started"] = True
            b.dma("sync", gchunk[:, cb, :], gate_d[i], "qS")
            st["ld_tick"][cb] = hand()

        def c1_s1(j):
            st = chain1_state
            cb = j % 2
            b.wait("scalar", "sQ", st["ld_tick"][cb])
            b.wait("scalar", "sS", st["i32t_free"])
            b.barrier("scalar", ["sV"])
            st["s"] = b.op("scalar", lambda e, cb=cb: e.activation(
                out=i32t, in_=gchunk[:, cb, :], func=AF.Identity, scale=rvec[1]),
                "sS")

        def c1_s2(j):
            st = chain1_state
            b.wait("scalar", "sS", st["s"])
            st["s"] = st["i32t_free"] = b.op("scalar", lambda e: e.activation(
                out=gq, in_=i32t, func=AF.Identity, scale=svec[1]), "sS")

        def c1_s3(j):
            st = chain1_state
            b.wait("scalar", "sS", st["s"])
            st["s"] = b.op("scalar", lambda e: e.activation(
                out=sg2, in_=gq, func=AF.Sigmoid), "sS")

        def c1_s4v(j):
            st = chain1_state
            cb = j % 2
            b.wait("scalar", "sS", st["s"])
            b.wait("scalar", "sV", st["u16_v"])
            sv = b.op("scalar", lambda e: e.activation(
                out=i32u, in_=sg2, func=AF.Identity, scale=65536.0), "sS")
            b.wait("vector", "sS", sv)
            vv = b.op("vector", lambda e: e.tensor_scalar(
                out=sgq, in0=i32u, scalar1=65535.0, scalar2=1.0 / 65536.0,
                op0=ALU.min, op1=ALU.mult), "sV")
            st["u16_v"] = vv
            b.wait("vector", "sV", vv)
            vv = b.op("vector", lambda e, cb=cb: e.tensor_tensor(
                out=gchunk[:, cb, :], in0=gq, in1=sgq, op=ALU.mult), "sV")
            b.wait("vector", "sV", vv)
            st["p2r_last"][cb] = b.op("vector", lambda e, j=j, cb=cb: e.tensor_reduce(
                out=gmaxc[:, j:j + 1], in_=gchunk[:, cb, :], axis=AX.X,
                op=ALU.max, apply_absolute_value=True), "sV")
            b.wait("sync", "sV", vv)
            b.dma("sync", gate_d[j], gchunk[:, cb, :], "qS")
            st["p2_last"][cb] = hand()

        def chain1_pump(i, th):
            if th == 0:
                c1_load(i)
                if i >= 1:
                    c1_s1(i - 1)
            elif th == 1:
                if i >= 1:
                    c1_s2(i - 1)
            elif th == 2:
                if i >= 1:
                    c1_s3(i - 1)
            else:
                if i >= 1:
                    c1_s4v(i - 1)

        def mm_updown(dst_d, maxc, wready, interleave=None):
            b.wait("tensor", "sQ", wready)
            b.wait("tensor", "sV", xq_ready)
            b.barrier("tensor", ["sS", "sV", "sGp"])
            b.barrier("gpsimd", ["sS"])
            pfree2 = [0, 0, 0, 0]
            pe2 = [None] * 4
            gl = [0, 0, 0, 0]
            sj = 0
            gh_tick = [0, 0, 0, 0]
            for i in range(24):
                for th in range(4):
                    sl = sj % 4
                    if pe2[sl] is not None:
                        b.wait("tensor", pe2[sl], pfree2[sl])
                    for c in range(8):
                        mv = b.op("tensor", lambda e, i=i, th=th, c=c, sl=sl: e.matmul(
                            out=pslot[sl], lhsT=wT[:, c, i * 128:(i + 1) * 128],
                            rhs=xqT[:, c, th * 512:(th + 1) * 512],
                            start=(c == 0), stop=(c == 7)), "sT" if c == 7 else None)
                    gb = sj % 4
                    b.wait("scalar", "sT", mv)
                    b.wait("scalar", "sQg", gh_tick[gb])
                    b.wait("scalar", "sV", gl[gb])
                    ev = b.op("scalar", lambda e, sl=sl, gb=gb: e.activation(
                        out=gstg[:, gb, :], in_=pslot[sl], func=AF.Identity), "sS")
                    pfree2[sl], pe2[sl] = ev, "sS"
                    b.wait("vector", "sS", ev)
                    gl[gb] = b.op("vector", lambda e, gb=gb, col=sj, maxc=maxc:
                                  e.tensor_reduce(
                        out=maxc[:, col:col + 1], in_=gstg[:, gb, :], axis=AX.X,
                        op=ALU.max, apply_absolute_value=True), "sV")
                    b.wait("gpsimd", "sS", ev)
                    b.dma("gpsimd", dst_d[i, :, th * 512:(th + 1) * 512],
                          gstg[:, gb, :], "qG")
                    if gb == 3:
                        hv = ghand()
                        gh_tick[0] = gh_tick[1] = gh_tick[2] = gh_tick[3] = hv
                    sj += 1
                    if interleave is not None:
                        interleave(i, th)
            return b.op("vector", lambda e: e.nop(), "sV")

        wready = load_wT(0)
        gdone = mm_updown(gate_d, gmaxc, wready)
        chain1_state["gate_store_done"] = b.cnt["sQg"]
        b.wait("vector", "sV", gdone)
        vv = b.op("vector", lambda e: e.tensor_reduce(
            out=gq[:, 0:1], in_=gmaxc, axis=AX.X, op=ALU.max), "sV")
        b.wait("vector", "sV", vv)
        fold_max(gq[:, 0:1], 1)
        allreduce_scales(1, [(1, 1)])

        b.barrier("sync", ["sT"])
        wready = load_wT(1)
        udone = mm_updown(up_d, umaxc, wready, interleave=chain1_pump)
        b.barrier("sync", ["sV", "sS", "sT"])
        c1_s1(23)
        c1_s2(23)
        c1_s3(23)
        c1_s4v(23)
        up_store_done = b.cnt["qS"]

        b.barrier("vector", ["sV", "sS"])
        vv = b.op("vector", lambda e: e.tensor_reduce(
            out=gq[:, 0:1], in_=umaxc, axis=AX.X, op=ALU.max), "sV")
        b.wait("vector", "sV", vv)
        fold_max(gq[:, 0:1], 2)
        vv = b.op("vector", lambda e: e.tensor_reduce(
            out=gq[:, 0:1], in_=gmaxc[:, 0:24], axis=AX.X, op=ALU.max), "sV")
        b.wait("vector", "sV", vv)
        fold_max(gq[:, 0:1], 3)
        allreduce_scales(2, [(2, 2), (3, 3)])

        wdT = wT.rearrange("p c o -> p (c o)")[:, 0:24 * 1024].rearrange(
            "p (i o) -> p i o", i=24)
        wd_view_g = [wq_g[R, 2].rearrange("c p o -> (c p o)").rearrange(
            "(i p o) -> i p o", i=24, p=128) for R in range(8)]

        s23 = af.alloc((1,))
        vv = b.op("vector", lambda e: e.tensor_scalar(
            out=s23, in0=svec[3], scalar1=svec[2], scalar2=None, op0=ALU.mult), "sV")
        b.wait("scalar", "sV", vv)
        b.wait("sync", "qS", up_store_done)
        upst_tick = hand()
        b.barrier("sync", ["sV", "sS"])
        b.wait("gpsimd", "sQ", upst_tick)
        b.barrier("gpsimd", ["sT", "sV", "sS"])
        gq2 = [gq, af.alloc((2048,))]
        Ubuf = [xqT[:, j, :] for j in range(4)]
        Pbuf = [xqT[:, 4 + j, :] for j in range(4)]
        Obuf = [gchunk[:, j, :] for j in range(4)]
        u_read = [0, 0, 0, 0]
        p_read = [0, 0, 0, 0]
        g_tt = [0, 0]
        o_store = [0, 0, 0, 0]
        utick = [0] * 24
        ptick = [0] * 24
        prev_tt = 0
        prev_s2 = 0
        p3r_last = 0

        pend_load = []
        pend_store = []

        def load_pair(j):
            sl = j % 4
            b.wait("gpsimd", "sV", u_read[sl])
            b.dma("gpsimd", Ubuf[sl], up_d[j], "qG")
            b.wait("gpsimd", "sS", p_read[sl])
            b.dma("gpsimd", Pbuf[sl], gate_d[j], "qG")
            pend_load.append(j)

        def tick_all():
            if pend_load:
                gt = ghand()
                for j in pend_load:
                    utick[j] = ptick[j] = gt
                pend_load.clear()
            if pend_store:
                pt = hand()
                for si in pend_store:
                    o_store[si % 4] = pt
                pend_store.clear()

        for j in range(4):
            load_pair(j)
        tick_all()
        wd_pairs = [(R, wi) for R in range(8) for wi in range(24)]
        for i in range(24):
            sl = i % 4
            cb = i % 2
            cbo = i % 4
            b.wait("scalar", "sQg", ptick[i])
            b.wait("scalar", "sS", prev_s2)
            s1 = b.op("scalar", lambda e, sl=sl: e.activation(
                out=i32t, in_=Pbuf[sl], func=AF.Identity, scale=rvec[3]), "sS")
            p_read[sl] = s1
            b.wait("scalar", "sS", s1)
            b.wait("scalar", "sV", g_tt[cb])
            prev_s2 = b.op("scalar", lambda e, cb=cb: e.activation(
                out=gq2[cb], in_=i32t, func=AF.Identity, scale=s23), "sS")
            b.wait("vector", "sQg", utick[i])
            b.wait("vector", "sV", prev_tt)
            v1 = b.op("vector", lambda e, sl=sl: e.tensor_scalar(
                out=i32u, in0=Ubuf[sl], scalar1=rvec[2], scalar2=None,
                op0=ALU.mult), "sV")
            u_read[sl] = v1
            b.wait("vector", "sV", v1)
            b.wait("vector", "sS", prev_s2)
            b.wait("vector", "sQ", o_store[cbo])
            tt = b.op("vector", lambda e, cb=cb, cbo=cbo: e.tensor_tensor(
                out=Obuf[cbo], in0=i32u, in1=gq2[cb], op=ALU.mult), "sV")
            g_tt[cb] = tt
            prev_tt = tt
            b.wait("vector", "sV", tt)
            p3r_last = b.op("vector", lambda e, i=i, cbo=cbo: e.tensor_reduce(
                out=umaxc[:, 24 + i:25 + i], in_=Obuf[cbo], axis=AX.X,
                op=ALU.max, apply_absolute_value=True), "sV")
            b.wait("sync", "sV", tt)
            b.dma("sync", up_d[i], Obuf[cbo], "qS")
            pend_store.append(i)
            if i + 4 < 24:
                load_pair(i + 4)
            if i == 0:
                b.barrier("sync", ["sT"])
            for Ri in wd_pairs[i * 8:(i + 1) * 8]:
                R, wi = Ri
                b.dma("sync", wdT[:, wi, R * 128:(R + 1) * 128],
                      wd_view_g[R][wi].rearrange("p o -> p o"), "qS")
            if i % 2 == 1:
                tick_all()
        tick_all()
        wd_ready = hand()

        b.wait("vector", "sV", p3r_last)
        vv = b.op("vector", lambda e: e.tensor_reduce(
            out=gq[:, 0:1], in_=umaxc[:, 24:48], axis=AX.X, op=ALU.max), "sV")
        b.wait("vector", "sV", vv)
        fold_max(gq[:, 0:1], 4)
        allreduce_scales(3, [(4, 4)])


        dslot = [psV[:, 0:512], psV[:, 512:1024], psA[:, 0:512], psA[:, 512:1024]]
        dfree = [0, 0, 0, 0]
        dj = 0
        ocol_last = [0, 0]
        oq_ready = [0, 0]
        out_st = 0
        b.barrier("tensor", ["sS", "sV", "sGp"])

        def emit_ocol(kt_i):
            ob = kt_i % 2
            b.wait("sync", "sT", ocol_last[ob])
            b.dma("sync", ocol[:, ob, :, :],
                  up_d[:, :, kt_i * 128:(kt_i + 1) * 128].rearrange(
                      "i p t -> p i t"), "qS")
            lv = hand()
            b.wait("vector", "sQ", lv)
            b.wait("vector", "sV", max(oq_ready[0], oq_ready[1]))
            vv = b.op("vector", lambda e, ob=ob: e.tensor_scalar(
                out=i32t, in0=ocol[:, ob, 0:16, :].rearrange("p i t -> p (i t)"),
                scalar1=rvec[4], scalar2=None, op0=ALU.mult), "sV")
            b.wait("vector", "sV", vv)
            vv = b.op("vector", lambda e, ob=ob: e.tensor_scalar(
                out=ocol[:, ob, 0:16, :].rearrange("p i t -> p (i t)"), in0=i32t,
                scalar1=svec[4], scalar2=None, op0=ALU.mult), "sV")
            b.wait("vector", "sV", vv)
            vv = b.op("vector", lambda e, ob=ob: e.tensor_scalar(
                out=i32t[:, 0:1024], in0=ocol[:, ob, 16:24, :].rearrange("p i t -> p (i t)"),
                scalar1=rvec[4], scalar2=None, op0=ALU.mult), "sV")
            b.wait("vector", "sV", vv)
            oq_ready[ob] = b.op("vector", lambda e, ob=ob: e.tensor_scalar(
                out=ocol[:, ob, 16:24, :].rearrange("p i t -> p (i t)"), in0=i32t[:, 0:1024],
                scalar1=svec[4], scalar2=None, op0=ALU.mult), "sV")

        b.barrier("vector", ["sS"])
        emit_ocol(0)
        for kt_i in range(NT):
            ob = kt_i % 2
            b.wait("tensor", "sV", oq_ready[ob])
            b.wait("tensor", "sQ", wd_ready)
            for h in range(2):
                sl = dj % 4
                b.wait("tensor", "sS", dfree[sl])
                for i in range(24):
                    mv = b.op("tensor", lambda e, h=h, i=i, sl=sl, ob=ob: e.matmul(
                        out=dslot[sl], lhsT=ocol[:, ob, i, :],
                        rhs=wdT[:, i, h * 512:(h + 1) * 512],
                        start=(i == 0), stop=(i == 23)), "sT" if i == 23 else None)
                b.wait("scalar", "sT", mv)
                ev = b.op("scalar", lambda e, sl=sl, h=h: e.activation(
                    out=oout[:, h * 512:(h + 1) * 512], in_=dslot[sl],
                    func=AF.Identity), "sS")
                dfree[sl] = ev
                dj += 1
            ocol_last[ob] = b.cnt["sT"]
            if kt_i + 1 < NT:
                emit_ocol(kt_i + 1)
            b.wait("sync", "sQ", out_st)
            b.dma("sync", xffn, x1_d[kt_i], "qS")
            lv = hand()
            b.wait("vector", "sS", dfree[(dj - 1) % 4])
            sv = ln_apply(oout, oout)
            b.wait("vector", "sS", sv)
            b.wait("vector", "sQ", lv)
            vv = b.op("vector", lambda e: e.tensor_tensor(
                out=xffn, in0=xffn, in1=oout, op=ALU.add), "sV")
            b.wait("sync", "sV", vv)
            b.dma("sync", out_d[kt_i], xffn, "qS")
            out_st = hand()
            b.wait("scalar", "sV", vv)

        b.wait("sync", "qS", b.cnt["qS"])
        af.pop(); ab.pop(); ai.pop()

        blk = es.enter_context(nc.Block())

        @blk.sync
        def _(e):
            for f in b.prog["sync"]:
                f(e)

        @blk.tensor
        def _(e):
            for f in b.prog["tensor"]:
                f(e)

        @blk.vector
        def _(e):
            for f in b.prog["vector"]:
                f(e)

        @blk.scalar
        def _(e):
            for f in b.prog["scalar"]:
                f(e)

        @blk.gpsimd
        def _(e):
            for f in b.prog["gpsimd"]:
                f(e)

    return nc


_NC_CACHE = None


def kernel(L0_attn_x, L0_attn_W, L0_attn_b, L0_ffn_w_gate, L0_ffn_w_up, L0_ffn_w_down):
    global _NC_CACHE
    x = np.ascontiguousarray(np.asarray(L0_attn_x, dtype=np.float32)).reshape(16384, D)
    W = np.ascontiguousarray(np.asarray(L0_attn_W, dtype=np.float32))
    bb = np.asarray(L0_attn_b, dtype=np.float32)
    wg = np.ascontiguousarray(np.asarray(L0_ffn_w_gate, dtype=np.float32))
    wu = np.ascontiguousarray(np.asarray(L0_ffn_w_up, dtype=np.float32))
    wd = np.ascontiguousarray(np.asarray(L0_ffn_w_down, dtype=np.float32))

    xt = x.reshape(128, 128, D)
    b_q = np.ascontiguousarray(bb[0:1024].reshape(8, 128).T)
    b_k = np.ascontiguousarray(bb[1024:2048].reshape(8, 128).T)
    b_v = np.ascontiguousarray(bb[2048:3072].reshape(1, 1024))
    identb = np.eye(128, dtype=ml_dtypes.bfloat16)
    identf = np.eye(128, dtype=np.float32)
    onesf = np.ones((1, 128), dtype=np.float32)

    in_maps = []
    for r in range(8):
        q = np.arange(128)[:, None]
        kk = np.arange(1024)[None, :]
        m = (kk <= 128 * r + q).astype(np.float32)
        mneg = (m - 1.0) * 1e30
        in_maps.append({
            "x_own": np.ascontiguousarray(xt[r::8]),
            "W": W,
            "b_q": b_q, "b_k": b_k, "b_v": b_v,
            "wg_sl": np.ascontiguousarray(
                wg[384 * r:384 * (r + 1)].reshape(3, 128, D).transpose(1, 0, 2)),
            "wu_sl": np.ascontiguousarray(
                wu[384 * r:384 * (r + 1)].reshape(3, 128, D).transpose(1, 0, 2)),
            "wd_sl": np.ascontiguousarray(wd[128 * r:128 * (r + 1)]),
            "mask1": m, "maskneg": mneg,
            "ident_bf": identb, "ident_f": identf, "ones_f": onesf,
        })
    if _NC_CACHE is None:
        _NC_CACHE = build()
    res = run_bass_kernel_spmd(_NC_CACHE, in_maps, core_ids=list(range(8)))
    global _LAST_RES
    _LAST_RES = res
    y = np.zeros((128, 128, D), dtype=np.float32)
    for r in range(8):
        y[r::8] = res.results[r]["out_own"]
    return y.reshape(2, 8192, D)


# version 19
# speedup vs baseline: 1.1373x; 1.1373x over previous
import contextlib
import numpy as np
import ml_dtypes
import concourse.bass as bass
import concourse.mybir as mybir
from concourse.bass_utils import run_bass_kernel_spmd

F32 = mybir.dt.float32
BF16 = mybir.dt.bfloat16
I32 = mybir.dt.int32
AF = mybir.ActivationFunctionType
ALU = mybir.AluOpType
AX = mybir.AxisListType
RG = [[0, 1, 2, 3, 4, 5, 6, 7]]

QMAX = 32767.0
WQMAX = 7.0
NT = 16
D = 1024
I3 = 3072
EPS = 1e-5
SSCALE = 0.03125


class Bld:
    def __init__(self, nc):
        self.nc = nc
        self.prog = {e: [] for e in ("sync", "tensor", "vector", "scalar", "gpsimd")}
        self.cnt = {}
        self.sems = {}

    def sem(self, name, es):
        self.sems[name] = es.enter_context(self.nc.semaphore(name))
        self.cnt[name] = 0

    def emit(self, eng, fn):
        self.prog[eng].append(fn)

    def wait(self, eng, sem, val):
        if val is None or val <= 0:
            return
        h = self.sems[sem]
        self.emit(eng, lambda e, h=h, v=val: e.wait_ge(h, v))

    def barrier(self, eng, sems):
        for s in sems:
            self.wait(eng, s, self.cnt[s])

    def dma(self, eng, out, in_, sem):
        h = self.sems[sem]
        self.cnt[sem] += 16
        self.emit(eng, lambda e, o=out, i=in_, h=h: e.dma_start(out=o, in_=i).then_inc(h, 16))
        return self.cnt[sem]

    def op(self, eng, fn, sem=None):
        if sem is None:
            self.emit(eng, fn)
            return None
        h = self.sems[sem]
        self.cnt[sem] += 1
        self.emit(eng, lambda e, f=fn, h=h: f(e).then_inc(h, 1))
        return self.cnt[sem]


class Arena:
    def __init__(self, nc, name, n, dtype, es):
        self.t = es.enter_context(nc.sbuf_tensor(name, [128, n], dtype))
        self.n = n
        self.off = 0
        self.marks = []

    def alloc(self, shape):
        sz = int(np.prod(shape))
        assert self.off + sz <= self.n, f"arena {self.t.name} overflow {self.off}+{sz}>{self.n}"
        ap = self.t[:, self.off:self.off + sz]
        if len(shape) > 1:
            letters = "abcdefg"[:len(shape)]
            pat = "p (" + " ".join(letters) + ") -> p " + " ".join(letters)
            ap = ap.rearrange(pat, **{letters[i]: shape[i] for i in range(len(shape) - 1)})
        self.off += sz
        return ap

    def push(self):
        self.marks.append(self.off)

    def pop(self):
        self.off = self.marks.pop()


def build():
    nc = bass.Bass("TRN2", target_bir_lowering=False)

    x_in = nc.dram_tensor("x_own", [NT, 128, D], F32, kind="ExternalInput").ap()
    W_in = nc.dram_tensor("W", [D, 3 * D], F32, kind="ExternalInput").ap()
    bq_in = nc.dram_tensor("b_q", [128, 8], F32, kind="ExternalInput").ap()
    bk_in = nc.dram_tensor("b_k", [128, 8], F32, kind="ExternalInput").ap()
    bv_in = nc.dram_tensor("b_v", [1, D], F32, kind="ExternalInput").ap()
    wg_in = nc.dram_tensor("wg_sl", [128, 3, D], F32, kind="ExternalInput").ap()
    wu_in = nc.dram_tensor("wu_sl", [128, 3, D], F32, kind="ExternalInput").ap()
    wd_in = nc.dram_tensor("wd_sl", [128, I3], F32, kind="ExternalInput").ap()
    mk_in = nc.dram_tensor("mask1", [128, 1024], F32, kind="ExternalInput").ap()
    mkn_in = nc.dram_tensor("maskneg", [128, 1024], F32, kind="ExternalInput").ap()
    idb_in = nc.dram_tensor("ident_bf", [128, 128], BF16, kind="ExternalInput").ap()
    idf_in = nc.dram_tensor("ident_f", [128, 128], F32, kind="ExternalInput").ap()
    ones_in = nc.dram_tensor("ones_f", [1, 128], F32, kind="ExternalInput").ap()
    out_d = nc.dram_tensor("out_own", [NT, 128, D], F32, kind="ExternalOutput").ap()


    kt_own2 = [nc.dram_tensor(f"kt_own{bi}", [8, 128, 8, 128], BF16, kind="Internal").ap()
               for bi in range(2)]
    kt_g2 = [nc.dram_tensor(f"kt_g{bi}", [8, 8, 128, 8, 128], BF16, kind="Internal",
                            addr_space="Shared").ap()
             for bi in range(2)]
    v_own2 = [nc.dram_tensor(f"v_own{bi}", [8, 128, D], BF16, kind="Internal").ap()
              for bi in range(2)]
    v_g2 = [nc.dram_tensor(f"v_g{bi}", [8, 8, 128, D], BF16, kind="Internal",
                           addr_space="Shared").ap()
            for bi in range(2)]
    wq_own = nc.dram_tensor("wq_own", [3, 8, 128, 384], BF16, kind="Internal").ap()
    wq_g = nc.dram_tensor("wq_g", [8, 3, 8, 128, 384], BF16, kind="Internal",
                          addr_space="Shared").ap()
    xlnT_d = nc.dram_tensor("xlnT_d", [8, 128, NT * 128], BF16, kind="Internal").ap()
    x1_d = nc.dram_tensor("x1_d", [NT, 128, D], F32, kind="Internal").ap()
    gate_d = nc.dram_tensor("gate_d", [24, 128, NT * 128], BF16, kind="Internal").ap()
    up_d = nc.dram_tensor("up_d", [24, 128, NT * 128], BF16, kind="Internal").ap()
    cc_in = [nc.dram_tensor(f"cc_in{i}", [1, 64], F32, kind="Internal").ap() for i in range(4)]
    cc_out = [nc.dram_tensor(f"cc_out{i}", [1, 64], F32, kind="Internal",
                             addr_space="Shared").ap() for i in range(4)]

    es = contextlib.ExitStack()
    with es:
        af = Arena(nc, "af", 17400, F32, es)
        ab = Arena(nc, "ab", 60544, BF16, es)
        ai = Arena(nc, "ai", 4096, I32, es)

        masks = af.alloc((1024,))
        masksn = af.alloc((1024,))
        bvb = af.alloc((D,))
        bq = af.alloc((8,))
        bk = af.alloc((8,))
        identf = af.alloc((128,))
        onesf = af.alloc((128,))
        maxcols = af.alloc((NT,))
        ccs = af.alloc((64,))
        ccr = af.alloc((64,))
        svec = [af.alloc((1,)) for _ in range(5)]
        rvec = [af.alloc((1,)) for _ in range(5)]
        ln = {k: af.alloc((1,)) for k in
              ("sum", "ssq", "mean", "msq", "var", "std", "rs", "nms")}
        lnx = [af.alloc((1,)) for _ in range(2)]
        sqd = af.alloc((1024,))
        identb = ab.alloc((128,))

        psA = es.enter_context(nc.psum_tensor("psA", [128, 1024], F32))
        psB = es.enter_context(nc.psum_tensor("psB", [128, 1024], F32))
        psV = es.enter_context(nc.psum_tensor("psV", [128, 1024], F32))
        psT = es.enter_context(nc.psum_tensor("psT", [128, 1024], BF16))
        psM = es.enter_context(nc.psum_tensor("psM", [128, 512], F32))

        b = Bld(nc)
        for s in ("qS", "qG", "sT", "sV", "sS", "sGp", "qKV0", "qKV1", "qW", "sQ",
                  "qC", "sQg"):
            b.sem(s, es)

        def hand():
            hq = b.sems["qS"]
            vq = b.cnt["qS"]
            b.emit("sync", lambda e, h=hq, v=vq: e.wait_ge(h, v))
            hs = b.sems["sQ"]
            b.cnt["sQ"] += 1
            b.emit("sync", lambda e, h=hs: e.nop().then_inc(h, 1))
            return b.cnt["sQ"]

        def ghand():
            hq = b.sems["qG"]
            vq = b.cnt["qG"]
            b.emit("gpsimd", lambda e, h=hq, v=vq: e.wait_ge(h, v))
            hs = b.sems["sQg"]
            b.cnt["sQg"] += 1
            b.emit("gpsimd", lambda e, h=hs: e.nop().then_inc(h, 1))
            return b.cnt["sQg"]

        def ln_apply(tile_f32, out_ap):
            b.barrier("scalar", ["sV", "sS"])
            b.barrier("vector", ["sS", "sV"])
            vs = b.op("scalar", lambda e: e.activation(
                out=sqd, in_=tile_f32, func=AF.Square, accum_out=ln["ssq"]), "sS")
            vv = b.op("vector", lambda e: e.tensor_reduce(
                out=ln["sum"], in_=tile_f32, axis=AX.X, op=ALU.add), "sV")
            b.wait("vector", "sS", vs)
            b.wait("vector", "sV", vv)
            b.op("vector", lambda e: e.tensor_scalar(
                out=ln["mean"], in0=ln["sum"], scalar1=1.0 / D, scalar2=None, op0=ALU.mult))
            vv = b.op("vector", lambda e: e.tensor_scalar(
                out=ln["msq"], in0=ln["ssq"], scalar1=1.0 / D, scalar2=None, op0=ALU.mult), "sV")
            b.wait("vector", "sV", vv)
            vv = b.op("vector", lambda e: e.tensor_scalar(
                out=ln["var"], in0=ln["mean"], scalar1=ln["mean"], scalar2=None, op0=ALU.mult), "sV")
            b.wait("vector", "sV", vv)
            vv = b.op("vector", lambda e: e.scalar_tensor_tensor(
                out=ln["var"], in0=ln["msq"], scalar=EPS, in1=ln["var"],
                op0=ALU.add, op1=ALU.subtract), "sV")
            b.wait("vector", "sV", vv)
            b.wait("scalar", "sV", vv)
            vs = b.op("scalar", lambda e: e.activation(
                out=ln["std"], in_=ln["var"], func=AF.Sqrt), "sS")
            b.wait("vector", "sS", vs)
            vv = b.op("vector", lambda e: e.reciprocal(out=ln["rs"], in_=ln["std"]), "sV")
            b.wait("vector", "sV", vv)
            vv = b.op("vector", lambda e: e.tensor_scalar(
                out=ln["nms"], in0=ln["mean"], scalar1=ln["rs"], scalar2=-1.0,
                op0=ALU.mult, op1=ALU.mult), "sV")
            b.wait("scalar", "sV", vv)
            return b.op("scalar", lambda e, o=out_ap, t=tile_f32: e.activation(
                out=o, in_=t, func=AF.Identity, scale=ln["rs"], bias=ln["nms"]), "sS")

        def ln_apply_recl(tile_f32, out_ap, recl_ap, lrun_col):
            b.barrier("scalar", ["sV", "sS"])
            b.barrier("vector", ["sS", "sV"])
            rr = b.op("vector", lambda e: e.reciprocal(
                out=recl_ap, in_=lrun_col), "sV")
            vs = b.op("scalar", lambda e: e.activation(
                out=sqd, in_=tile_f32, func=AF.Square, accum_out=ln["ssq"]), "sS")
            vv = b.op("vector", lambda e: e.tensor_reduce(
                out=ln["sum"], in_=tile_f32, axis=AX.X, op=ALU.add), "sV")
            b.wait("vector", "sS", vs)
            b.wait("vector", "sV", max(vv, rr))
            b.op("vector", lambda e: e.tensor_scalar(
                out=ln["mean"], in0=ln["sum"], scalar1=recl_ap, scalar2=1.0 / D,
                op0=ALU.mult, op1=ALU.mult))
            vv = b.op("vector", lambda e: e.tensor_scalar(
                out=lnx[0], in0=recl_ap, scalar1=recl_ap, scalar2=1.0 / D,
                op0=ALU.mult, op1=ALU.mult), "sV")
            b.wait("vector", "sV", vv)
            vv = b.op("vector", lambda e: e.tensor_scalar(
                out=ln["msq"], in0=ln["ssq"], scalar1=lnx[0], scalar2=None,
                op0=ALU.mult), "sV")
            b.wait("vector", "sV", vv)
            vv = b.op("vector", lambda e: e.tensor_scalar(
                out=ln["var"], in0=ln["mean"], scalar1=ln["mean"], scalar2=None, op0=ALU.mult), "sV")
            b.wait("vector", "sV", vv)
            vv = b.op("vector", lambda e: e.scalar_tensor_tensor(
                out=ln["var"], in0=ln["msq"], scalar=EPS, in1=ln["var"],
                op0=ALU.add, op1=ALU.subtract), "sV")
            b.wait("vector", "sV", vv)
            b.wait("scalar", "sV", vv)
            vs = b.op("scalar", lambda e: e.activation(
                out=ln["std"], in_=ln["var"], func=AF.Sqrt), "sS")
            b.wait("vector", "sS", vs)
            vv = b.op("vector", lambda e: e.reciprocal(out=ln["rs"], in_=ln["std"]), "sV")
            b.wait("vector", "sV", vv)
            b.op("vector", lambda e: e.tensor_scalar(
                out=ln["nms"], in0=ln["mean"], scalar1=ln["rs"], scalar2=-1.0,
                op0=ALU.mult, op1=ALU.mult))
            vv = b.op("vector", lambda e: e.tensor_scalar(
                out=lnx[1], in0=ln["rs"], scalar1=recl_ap, scalar2=None,
                op0=ALU.mult), "sV")
            b.wait("scalar", "sV", vv)
            return b.op("scalar", lambda e, o=out_ap, t=tile_f32: e.activation(
                out=o, in_=t, func=AF.Identity, scale=lnx[1], bias=ln["nms"]), "sS")

        def fold_max(src_col, col):
            b.barrier("tensor", ["sV", "sGp"])
            vt = b.op("tensor", lambda e, s=src_col: e.matmul(
                out=psM[0:1, 0:128], lhsT=s, rhs=identf, start=True, stop=True), "sT")
            b.wait("vector", "sT", vt)
            vv = b.op("vector", lambda e, c=col: e.tensor_reduce(
                out=ccs[0:1, c:c + 1], in_=psM[0:1, 0:128], axis=AX.X,
                op=ALU.max, apply_absolute_value=True), "sV")
            b.wait("tensor", "sV", vv)
            return vv

        def allreduce_scales(round_i, cols):
            b.barrier("gpsimd", ["sV"])
            wv = b.dma("gpsimd", cc_in[round_i], ccs[0:1, :], "qG")
            b.wait("gpsimd", "qG", wv)
            cv = b.op("gpsimd", lambda e, i=round_i: e.collective_compute(
                "AllReduce", ALU.max, RG, ins=[cc_in[i].opt()], outs=[cc_out[i].opt()]), "qC")
            b.wait("gpsimd", "qC", cv)
            rv = b.dma("gpsimd", ccr[0:1, :], cc_out[round_i], "qG")
            b.wait("tensor", "qG", rv)
            for col, idx in cols:
                vt = b.op("tensor", lambda e, c=col: e.matmul(
                    out=psM[:, 0:1], lhsT=onesf[0:1, :], rhs=ccr[0:1, c:c + 1],
                    start=True, stop=True), "sT")
                b.wait("vector", "sT", vt)
                vv = b.op("vector", lambda e, i=idx: e.tensor_scalar(
                    out=svec[i], in0=psM[:, 0:1], scalar1=1.0 / QMAX, scalar2=1e-12,
                    op0=ALU.mult, op1=ALU.max), "sV")
                b.wait("vector", "sV", vv)
                vv = b.op("vector", lambda e, i=idx: e.reciprocal(
                    out=rvec[i], in_=svec[i]), "sV")
                b.wait("tensor", "sV", vv)
            vv = b.cnt["sV"]
            for eng in ("vector", "gpsimd", "scalar"):
                b.wait(eng, "sV", vv)
            return vv

        b.dma("sync", masks, mk_in, "qS")
        b.dma("sync", masksn, mkn_in, "qS")
        b.dma("sync", bq, bq_in, "qS")
        b.dma("sync", bk, bk_in, "qS")
        b.dma("sync", identf, idf_in, "qS")
        b.dma("sync", onesf[0:1, :], ones_in, "qS")
        b.dma("sync", identb, idb_in, "qS")
        b.op("vector", lambda e: e.memset(ccs[0:1, :], 0.0), "sV")

        af.push()
        bvrow = af.alloc((D,))

        b.dma("sync", bvrow[0:1, :], bv_in, "qS")
        cv0 = hand()
        b.wait("tensor", "sQ", cv0)
        for h in range(2):
            vt = b.op("tensor", lambda e, h=h: e.matmul(
                out=psM[:, 0:512], lhsT=onesf[0:1, :], rhs=bvrow[0:1, 512 * h:512 * (h + 1)],
                start=True, stop=True), "sT")
            b.wait("vector", "sT", vt)
            vv = b.op("vector", lambda e, h=h: e.tensor_copy(
                out=bvb[:, 512 * h:512 * (h + 1)], in_=psM[:, 0:512]), "sV")
            b.wait("tensor", "sV", vv)
        af.pop()

        af.push(); ab.push(); ai.push()
        qT = ab.alloc((8, 2048))

        af.push(); ab.push()
        b.barrier("sync", ["sT", "sV", "sS"])
        W_sb = ab.alloc((8, 3 * D))
        xlnT = ab.alloc((8, 2048))
        xlnb = ab.alloc((D,))
        kvst = ab.alloc((4, 512))
        Wstg = af.alloc((2, 3 * D))
        xst = af.alloc((2, D))

        last_cast = [0, 0]
        for c in range(8):
            buf = c % 2
            if buf == 0:
                b.wait("sync", "sV", last_cast[buf])
                b.dma("sync", Wstg[:, buf, :], W_in[c * 128:(c + 1) * 128, :], "qS")
                tick, tsem = hand(), "sQ"
            else:
                b.wait("gpsimd", "sV", last_cast[buf])
                b.dma("gpsimd", Wstg[:, buf, :], W_in[c * 128:(c + 1) * 128, :], "qG")
                tick, tsem = ghand(), "sQg"
            b.wait("vector", tsem, tick)
            last_cast[buf] = b.op("vector", lambda e, c=c, buf=buf: e.tensor_copy(
                out=W_sb[:, c, :], in_=Wstg[:, buf, :]), "sV")

        xst_free = [0, 0]
        for k in range(NT):
            buf = k % 2
            b.wait("sync", "sS", xst_free[buf])
            b.dma("sync", xst[:, buf, :], x_in[k], "qS")
            lv = hand()
            b.wait("scalar", "sQ", lv)
            b.wait("vector", "sQ", lv)
            b.barrier("scalar", ["sT"])
            sv = ln_apply(xst[:, buf, :], xlnb)
            xst_free[buf] = sv
            b.wait("tensor", "sS", sv)
            b.barrier("tensor", ["sV"])
            for c in range(8):
                vt = b.op("tensor", lambda e, c=c: e.transpose(
                    out=psT[:, c * 128:(c + 1) * 128],
                    in_=xlnb[:, c * 128:(c + 1) * 128], identity=identb),
                    "sT" if c == 7 else None)
            b.wait("vector", "sT", vt)
            vv = b.op("vector", lambda e, k=k: e.tensor_copy(
                out=xlnT[:, :, k * 128:(k + 1) * 128],
                in_=psT[:].rearrange("p (c t) -> p c t", c=8)), "sV")
            b.wait("tensor", "sV", vv)

        pslot = [psA[:, 0:512], psA[:, 512:1024], psB[:, 0:512], psB[:, 512:1024]]
        pfree = [0, 0, 0, 0]
        pfeng = [None] * 4
        si = 0
        kst_hand = [0, 0, 0, 0]
        gather_done = {}
        hv = hand()
        b.wait("tensor", "sQ", hv)
        b.barrier("tensor", ["sV"])
        for th in range(4):
            for j in range(16):
                sl = si % 4
                if pfeng[sl] is not None:
                    b.wait("tensor", pfeng[sl], pfree[sl])
                for c in range(8):
                    mv = b.op("tensor", lambda e, j=j, th=th, c=c, sl=sl: e.matmul(
                        out=pslot[sl], lhsT=W_sb[:, c, j * 128:(j + 1) * 128],
                        rhs=xlnT[:, c, th * 512:(th + 1) * 512],
                        start=(c == 0), stop=(c == 7)), "sT" if c == 7 else None)
                if j < 8:
                    b.wait("scalar", "sT", mv)
                    ev = b.op("scalar", lambda e, j=j, th=th, sl=sl: e.activation(
                        out=qT[:, j, th * 512:(th + 1) * 512], in_=pslot[sl],
                        func=AF.Identity, bias=bq[:, j:j + 1]), "sS")
                    pfree[sl], pfeng[sl] = ev, "sS"
                else:
                    jc = j - 8
                    kb = si % 4
                    sbuf = kvst[:, kb, :]
                    b.wait("scalar", "sT", mv)
                    b.wait("scalar", "sQ", kst_hand[kb])
                    ev = b.op("scalar", lambda e, jc=jc, sl=sl, sbuf=sbuf: e.activation(
                        out=sbuf, in_=pslot[sl], func=AF.Identity,
                        bias=bk[:, jc:jc + 1]), "sS")
                    pfree[sl], pfeng[sl] = ev, "sS"
                    b.wait("sync", "sS", ev)
                    bi, t4 = th // 2, th % 2
                    b.dma(
                        "sync",
                        kt_own2[bi][t4 * 4:(t4 + 1) * 4, :, jc, :].rearrange(
                            "k p t -> p k t"),
                        sbuf.rearrange("p (k t) -> p k t", k=4), "qS")
                    if kb == 3:
                        hv = hand()
                        kst_hand[0] = kst_hand[1] = kst_hand[2] = kst_hand[3] = hv
                si += 1
            for kt_i in range(th * 4, (th + 1) * 4):
                for h in range(2):
                    sl = si % 4
                    if pfeng[sl] is not None:
                        b.wait("tensor", pfeng[sl], pfree[sl])
                    for c in range(8):
                        mv = b.op("tensor", lambda e, kt_i=kt_i, h=h, c=c, sl=sl: e.matmul(
                            out=pslot[sl],
                            lhsT=xlnT[:, c, kt_i * 128:(kt_i + 1) * 128],
                            rhs=W_sb[:, c, 2048 + h * 512:2048 + (h + 1) * 512],
                            start=(c == 0), stop=(c == 7)), "sT" if c == 7 else None)
                    kb = si % 4
                    sbuf = kvst[:, kb, :]
                    b.wait("vector", "sT", mv)
                    b.wait("vector", "sQ", kst_hand[kb])
                    ev = b.op("vector", lambda e, h=h, sl=sl, sbuf=sbuf: e.tensor_tensor(
                        out=sbuf, in0=pslot[sl], in1=bvb[:, h * 512:(h + 1) * 512],
                        op=ALU.add), "sV")
                    pfree[sl], pfeng[sl] = ev, "sV"
                    b.wait("sync", "sV", ev)
                    bi = kt_i // 8
                    b.dma("sync", v_own2[bi][kt_i % 8, :, h * 512:(h + 1) * 512],
                          sbuf, "qS")
                    if kb == 3:
                        hv = hand()
                        kst_hand[0] = kst_hand[1] = kst_hand[2] = kst_hand[3] = hv
                    si += 1
            if th == 1 or th == 3:
                bi = th // 2
                hv = hand()
                b.wait("gpsimd", "sQ", hv)
                b.op("gpsimd", lambda e, bi=bi: e.collective_compute(
                    "AllGather", ALU.bypass, RG,
                    ins=[kt_own2[bi].opt()], outs=[kt_g2[bi].opt()]), f"qKV{bi}")
                gather_done[bi] = b.op(
                    "gpsimd", lambda e, bi=bi: e.collective_compute(
                        "AllGather", ALU.bypass, RG,
                        ins=[v_own2[bi].opt()], outs=[v_g2[bi].opt()]), f"qKV{bi}")
        af.pop(); ab.pop()

        af.push(); ab.push(); ai.push()
        wsl = af.alloc((3 * D,))
        wabs = af.alloc((3 * D,))
        bmt = af.alloc((3, 1536))
        wq_bf = ab.alloc((3 * D,))
        wtr = ab.alloc((1024,))
        qi32 = ai.alloc((3 * D,))
        b.barrier("sync", ["sT", "sS", "sV"])

        def wqdq(n_free):
            nb = n_free // 32
            b.barrier("vector", ["sT"])
            vv = b.op("vector", lambda e, n=n_free: e.scalar_tensor_tensor(
                out=wabs[:, 0:n], in0=wsl[:, 0:n], scalar=-1.0, in1=wsl[:, 0:n],
                op0=ALU.mult, op1=ALU.max), "sV")
            b.wait("vector", "sV", vv)
            v3 = wabs[:, 0:n_free].rearrange("p (a two c) -> p a two c", a=nb, two=2)
            cur = bmt[:, 0, 0:nb * 16].rearrange("p (a c) -> p a c", a=nb)
            vv = b.op("vector", lambda e, cur=cur, v3=v3: e.tensor_tensor(
                out=cur, in0=v3[:, :, 0, :], in1=v3[:, :, 1, :], op=ALU.max), "sV")
            b.wait("vector", "sV", vv)
            w, buf = 16, 1
            while w > 1:
                nxt = bmt[:, buf, 0:nb * (w // 2)].rearrange("p (a c) -> p a c", a=nb)
                vv = b.op("vector", lambda e, cur=cur, nxt=nxt: e.tensor_tensor(
                    out=nxt, in0=cur[:, :, 0::2], in1=cur[:, :, 1::2], op=ALU.max), "sV")
                b.wait("vector", "sV", vv)
                cur, w, buf = nxt, w // 2, buf ^ 1
            sb_ = bmt[:, 2, 0:nb]
            rb_ = bmt[:, 2, 1536 - nb:1536]
            vv = b.op("vector", lambda e, cur=cur, sb_=sb_: e.tensor_scalar(
                out=sb_, in0=cur.rearrange("p a c -> p (a c)"),
                scalar1=1.0 / WQMAX, scalar2=1e-12, op0=ALU.mult, op1=ALU.max), "sV")
            b.wait("vector", "sV", vv)
            vv = b.op("vector", lambda e, sb_=sb_, rb_=rb_: e.reciprocal(
                out=rb_, in_=sb_), "sV")
            b.wait("vector", "sV", vv)
            vv = b.op("vector", lambda e, nb=nb, rb_=rb_: e.tensor_tensor(
                out=qi32[:, 0:n_free].rearrange("p (a c) -> p a c", a=nb),
                in0=wsl[:, 0:n_free].rearrange("p (a c) -> p a c", a=nb),
                in1=rb_.rearrange("p (a c) -> p a c", a=nb).broadcast_to((128, nb, 32)),
                op=ALU.mult), "sV")
            b.wait("vector", "sV", vv)
            return b.op("vector", lambda e, nb=nb, sb_=sb_: e.tensor_tensor(
                out=wq_bf[:, 0:n_free].rearrange("p (a c) -> p a c", a=nb),
                in0=qi32[:, 0:n_free].rearrange("p (a c) -> p a c", a=nb),
                in1=sb_.rearrange("p (a c) -> p a c", a=nb).broadcast_to((128, nb, 32)),
                op=ALU.mult), "sV")

        last_store = 0
        for widx, src in ((0, wg_in), (1, wu_in)):
            b.barrier("sync", ["sV"])
            b.dma("sync", wsl.rearrange("p (a c) -> p a c", a=3), src, "qS")
            lv = hand()
            b.wait("vector", "sQ", lv)
            qv = wqdq(3 * D)
            b.wait("tensor", "sV", qv)
            for c in range(8):
                for o3 in range(3):
                    vt = b.op("tensor", lambda e, c=c, o3=o3: e.transpose(
                        out=psT[:, o3 * 128:(o3 + 1) * 128],
                        in_=wq_bf[:, (o3 * D + c * 128):(o3 * D + (c + 1) * 128)],
                        identity=identb), "sT" if o3 == 2 else None)
                b.wait("vector", "sT", vt)
                b.wait("vector", "sQ", last_store)
                cvv = b.op("vector", lambda e: e.tensor_copy(
                    out=wtr[:, 0:384], in_=psT[:, 0:384]), "sV")
                b.wait("sync", "sV", cvv)
                b.dma("sync", wq_own[widx, c, :, :], wtr[:, 0:384], "qS")
                last_store = hand()
                b.wait("tensor", "sV", cvv)
        b.barrier("sync", ["sV"])
        b.dma("sync", wsl, wd_in, "qS")
        lv = hand()
        b.wait("vector", "sQ", lv)
        qv = wqdq(I3)
        b.wait("tensor", "sV", qv)
        wd_view = wq_own[2].rearrange("c p o -> (c p o)").rearrange(
            "(i p o) -> i p o", i=24, p=128)
        for i8 in range(3):
            for j in range(8):
                i = i8 * 8 + j
                vt = b.op("tensor", lambda e, i=i, j=j: e.transpose(
                    out=psT[:, j * 128:(j + 1) * 128],
                    in_=wq_bf[:, i * 128:(i + 1) * 128], identity=identb),
                    "sT" if j == 7 else None)
            b.wait("vector", "sT", vt)
            b.wait("vector", "sQ", last_store)
            cvv = b.op("vector", lambda e: e.tensor_copy(out=wtr, in_=psT[:]), "sV")
            b.wait("sync", "sV", cvv)
            b.dma("sync",
                  wd_view[i8 * 8:(i8 + 1) * 8, :, :].rearrange("i p o -> p i o"),
                  wtr.rearrange("p (i o) -> p i o", i=8), "qS")
            last_store = hand()
            b.wait("tensor", "sV", cvv)
        hv = hand()
        b.wait("gpsimd", "sQ", hv)
        wq_gather_done = b.op("gpsimd", lambda e: e.collective_compute(
            "AllGather", ALU.bypass, RG, ins=[wq_own.opt()], outs=[wq_g.opt()]), "qW")
        af.pop(); ab.pop(); ai.pop()

        acc = af.alloc((8, D))
        mrun = af.alloc((8,))
        lrun = af.alloc((8,))
        mnt = af.alloc((1,))
        mneg = af.alloc((1,))
        alpha = af.alloc((1,))
        cmax = af.alloc((1,))
        rowsum = af.alloc((1,))
        recl = af.alloc((1,))
        xst2 = af.alloc((D,))
        aout = af.alloc((D,))
        x1t = af.alloc((D,))
        ktc = ab.alloc((2, 8, 8, 128))
        vc = ab.alloc((2, 8, 1024))
        expb = ab.alloc((1024,))
        attnT = ab.alloc((1024,))
        xlnb2 = ab.alloc((D,))
        xlts2 = ab.alloc((8, 128))

        b.barrier("tensor", ["sS", "sV"])
        b.barrier("sync", ["sT", "sV", "sS"])

        xst2_free = 0
        x1t_store = 0
        xlts2_store = 0
        scoreps = [psA, psB]
        score_free = [0, 0]
        ktc_free = [0, 0]
        vc_free = [0, 0]
        ktc_ready = [0, 0]
        vc_ready = [0, 0]
        last_exp_consumed = 0
        last_attnT_used = 0
        last_acc_upd = 0

        for bat in range(2):
            b.wait("sync", f"qKV{bat}", gather_done[bat])
            visits = [(c, m) for c in range(8) for m in range(c, 8)]
            loads_done = {}
            mvs = {}

            def emit_loads(c):
                cb = c % 2
                b.wait("sync", "sT", ktc_free[cb])
                for r in range(8):
                    b.dma("sync", ktc[:, cb, r, :, :].rearrange("p c t -> p (c t)"),
                          kt_g2[bat][r, c].rearrange("p c t -> p (c t)"), "qS")
                b.wait("sync", "sT", vc_free[cb])
                for r in range(8):
                    b.dma("sync", vc[:, cb, r, :], v_g2[bat][r, c], "qS")
                wv = hand()
                ktc_ready[cb] = wv
                vc_ready[cb] = wv
                loads_done[c] = True

            def emit_A(j):
                c, m = visits[j]
                cb = c % 2
                if c not in loads_done:
                    emit_loads(c)
                sp = scoreps[m % 2]
                k_tile = bat * 8 + m
                b.wait("tensor", "sQ", ktc_ready[cb])
                b.wait("tensor", "sS", score_free[m % 2])
                for h in range(2):
                    for cc in range(8):
                        mv = b.op("tensor", lambda e, h=h, cc=cc, sp=sp, cb=cb, k_tile=k_tile: e.matmul(
                            out=sp[:, h * 512:(h + 1) * 512],
                            lhsT=qT[:, cc, k_tile * 128:(k_tile + 1) * 128],
                            rhs=ktc[:, cb, h * 4:(h + 1) * 4, cc, :],
                            start=(cc == 0), stop=(cc == 7)),
                            "sT" if (cc == 7 and h == 1) else None)
                mvs[j] = mv
                if m == 7:
                    ktc_free[cb] = mv

            def emit_B(j):
                nonlocal last_exp_consumed, last_attnT_used, last_acc_upd
                c, m = visits[j]
                cb = c % 2
                sp = scoreps[m % 2]
                mv = mvs[j]
                b.wait("vector", "sT", mv)
                if m == c:
                    vv = b.op("vector", lambda e, sp=sp: e.tensor_tensor(
                        out=sp[:], in0=sp[:], in1=masks, op=ALU.mult), "sV")
                    b.wait("vector", "sV", vv)
                    vv = b.op("vector", lambda e, sp=sp: e.tensor_tensor(
                        out=sp[:], in0=sp[:], in1=masksn, op=ALU.add), "sV")
                    b.wait("vector", "sV", vv)
                vv = b.op("vector", lambda e, sp=sp: e.tensor_reduce(
                    out=cmax, in_=sp[:], axis=AX.X, op=ALU.max), "sV")
                b.wait("vector", "sV", vv)
                first = (c == 0)
                if first:
                    b.op("vector", lambda e, m=m: e.tensor_copy(
                        out=mrun[:, m:m + 1], in_=cmax))
                    vv = b.op("vector", lambda e: e.tensor_scalar(
                        out=mneg, in0=cmax, scalar1=-SSCALE, scalar2=None,
                        op0=ALU.mult), "sV")
                    b.wait("scalar", "sV", vv)
                else:
                    vv = b.op("vector", lambda e, m=m: e.tensor_tensor(
                        out=mnt, in0=mrun[:, m:m + 1], in1=cmax, op=ALU.max), "sV")
                    b.wait("vector", "sV", vv)
                    vv = b.op("vector", lambda e: e.tensor_scalar(
                        out=mneg, in0=mnt, scalar1=-SSCALE, scalar2=None,
                        op0=ALU.mult), "sV")
                    b.wait("scalar", "sV", vv)
                    av = b.op("scalar", lambda e, m=m: e.activation(
                        out=alpha, in_=mrun[:, m:m + 1], func=AF.Exp,
                        scale=SSCALE, bias=mneg), "sS")
                    b.wait("vector", "sS", av)
                    b.op("vector", lambda e, m=m: e.tensor_copy(
                        out=mrun[:, m:m + 1], in_=mnt))
                b.wait("scalar", "sT", last_exp_consumed)
                sv = b.op("scalar", lambda e, sp=sp: e.activation(
                    out=expb, in_=sp[:], func=AF.Exp, scale=SSCALE, bias=mneg,
                    accum_out=rowsum), "sS")
                score_free[m % 2] = sv
                b.wait("tensor", "sS", sv)
                b.barrier("tensor", ["sV"])
                for r in range(8):
                    tv = b.op("tensor", lambda e, r=r: e.transpose(
                        out=psT[:, r * 128:(r + 1) * 128],
                        in_=expb[:, r * 128:(r + 1) * 128], identity=identb),
                        "sT" if r == 7 else None)
                last_exp_consumed = tv
                b.wait("vector", "sT", tv)
                b.wait("vector", "sT", last_attnT_used)
                cvv = b.op("vector", lambda e: e.tensor_copy(out=attnT, in_=psT[:]), "sV")
                b.wait("tensor", "sV", cvv)
                b.wait("tensor", "sQ", vc_ready[cb])
                b.wait("tensor", "sV", last_acc_upd)
                for r in range(8):
                    for h in range(2):
                        pv = b.op("tensor", lambda e, r=r, h=h, cb=cb: e.matmul(
                            out=psV[:, h * 512:(h + 1) * 512],
                            lhsT=attnT[:, r * 128:(r + 1) * 128],
                            rhs=vc[:, cb, r, h * 512:(h + 1) * 512],
                            start=(r == 0), stop=(r == 7)),
                            "sT" if (r == 7 and h == 1) else None)
                last_attnT_used = pv
                if m == 7:
                    vc_free[cb] = pv
                b.wait("vector", "sT", pv)
                b.wait("vector", "sS", sv)
                if first:
                    b.op("vector", lambda e, m=m: e.tensor_copy(
                        out=lrun[:, m:m + 1], in_=rowsum))
                    last_acc_upd = b.op("vector", lambda e, m=m: e.tensor_copy(
                        out=acc[:, m, :], in_=psV[:]), "sV")
                else:
                    b.op("vector", lambda e, m=m: e.tensor_scalar(
                        out=lrun[:, m:m + 1], in0=lrun[:, m:m + 1],
                        scalar1=alpha, scalar2=rowsum,
                        op0=ALU.mult, op1=ALU.add))
                    last_acc_upd = b.op("vector", lambda e, m=m: e.scalar_tensor_tensor(
                        out=acc[:, m, :], in0=acc[:, m, :], scalar=alpha,
                        in1=psV[:], op0=ALU.mult, op1=ALU.add), "sV")

            emit_A(0)
            nv = len(visits)
            for j in range(nv):
                if j + 1 < nv and (visits[j + 1][1] % 2) != (visits[j][1] % 2):
                    emit_A(j + 1)
                    emit_B(j)
                else:
                    emit_B(j)
                    if j + 1 < nv:
                        emit_A(j + 1)

            for m in range(8):
                k_tile = bat * 8 + m
                b.barrier("vector", ["sS"])
                b.barrier("scalar", ["sT"])
                b.wait("sync", "sV", xst2_free)
                b.dma("sync", xst2, x_in[k_tile], "qS")
                lv = hand()
                sv = ln_apply_recl(acc[:, m, :], aout, recl, lrun[:, m:m + 1])
                b.wait("vector", "sS", sv)
                b.wait("vector", "sQ", lv)
                b.wait("vector", "sQ", x1t_store)
                vv = b.op("vector", lambda e: e.tensor_tensor(
                    out=x1t, in0=xst2, in1=aout, op=ALU.add), "sV")
                xst2_free = vv
                b.wait("sync", "sV", vv)
                b.dma("sync", x1_d[k_tile], x1t, "qS")
                x1t_store = hand()
                b.wait("scalar", "sV", vv)
                sv = ln_apply(x1t, xlnb2)
                b.wait("vector", "sS", sv)
                b.op("vector", lambda e, k=k_tile: e.tensor_reduce(
                    out=maxcols[:, k:k + 1], in_=xlnb2, axis=AX.X,
                    op=ALU.max, apply_absolute_value=True), "sV")
                b.wait("tensor", "sS", sv)
                b.barrier("tensor", ["sV"])
                for c in range(8):
                    vt = b.op("tensor", lambda e, c=c: e.transpose(
                        out=psT[:, c * 128:(c + 1) * 128],
                        in_=xlnb2[:, c * 128:(c + 1) * 128], identity=identb),
                        "sT" if c == 7 else None)
                b.wait("vector", "sT", vt)
                b.wait("vector", "sQ", xlts2_store)
                cvv = b.op("vector", lambda e: e.tensor_copy(
                    out=xlts2.rearrange("p c t -> p (c t)"), in_=psT[:]), "sV")
                b.wait("sync", "sV", cvv)
                b.dma("sync", xlnT_d[:, :, k_tile * 128:(k_tile + 1) * 128].rearrange(
                    "c p t -> p c t"), xlts2, "qS")
                xlts2_store = hand()
                b.wait("tensor", "sV", cvv)

        b.barrier("vector", ["sV"])
        vv = b.op("vector", lambda e: e.tensor_reduce(
            out=cmax, in_=maxcols, axis=AX.X, op=ALU.max), "sV")
        b.wait("vector", "sV", vv)
        fold_max(cmax, 0)
        allreduce_scales(0, [(0, 0)])

        af.pop(); ab.pop(); ai.pop()

        af.push(); ab.push(); ai.push()
        b.barrier("sync", ["sT", "sV", "sS"])
        gmaxc = af.alloc((96,))
        umaxc = af.alloc((96,))
        gq = af.alloc((2048,))
        sg2 = af.alloc((2048,))
        xffn = af.alloc((D,))
        oout = af.alloc((D,))
        wT = ab.alloc((8, I3))
        xqT = ab.alloc((8, 2048))
        gstg = ab.alloc((4, 512))
        gchunk = ab.alloc((4, 2048))
        ocol = ab.alloc((2, 24, 128))
        i32t = ai.alloc((2048,))
        i32u = ai.alloc((2048,))

        b.dma("sync", xqT, xlnT_d.rearrange("c p t -> p c t"), "qS")
        lv = hand()
        b.wait("vector", "sQ", lv)
        for c in range(8):
            vv = b.op("vector", lambda e, c=c: e.tensor_scalar(
                out=i32t, in0=xqT[:, c, :], scalar1=rvec[0], scalar2=None, op0=ALU.mult), "sV")
            b.wait("vector", "sV", vv)
            vv = b.op("vector", lambda e, c=c: e.tensor_scalar(
                out=xqT[:, c, :], in0=i32t, scalar1=svec[0], scalar2=None, op0=ALU.mult),
                "sV")
            b.wait("vector", "sV", vv)
        xq_ready = b.cnt["sV"]

        b.wait("sync", "qW", wq_gather_done)

        def load_wT(widx):
            for R in range(8):
                for c in range(8):
                    b.dma("sync", wT[:, c, 384 * R:384 * (R + 1)],
                          wq_g[R, widx, c, :, :], "qS")
            return hand()

        sgq = af.alloc((2048,))

        chain1_state = {"p2_last": [0, 0], "p2r_last": [0, 0], "started": False,
                        "gate_store_done": 0, "ld_tick": [0, 0], "s": 0,
                        "i32t_free": 0, "u16_v": 0}

        def c1_load(i):
            st = chain1_state
            cb = i % 2
            b.wait("sync", "sQ", st["p2_last"][cb])
            b.wait("sync", "sV", st["p2r_last"][cb])
            if not st["started"]:
                b.wait("sync", "sQg", st["gate_store_done"])
                st["started"] = True
            b.dma("sync", gchunk[:, cb, :], gate_d[i], "qS")
            st["ld_tick"][cb] = hand()

        def c1_s1(j):
            st = chain1_state
            cb = j % 2
            b.wait("scalar", "sQ", st["ld_tick"][cb])
            b.wait("scalar", "sS", st["i32t_free"])
            b.barrier("scalar", ["sV"])
            st["s"] = b.op("scalar", lambda e, cb=cb: e.activation(
                out=i32t, in_=gchunk[:, cb, :], func=AF.Identity, scale=rvec[1]),
                "sS")

        def c1_s2(j):
            st = chain1_state
            b.wait("scalar", "sS", st["s"])
            st["s"] = st["i32t_free"] = b.op("scalar", lambda e: e.activation(
                out=gq, in_=i32t, func=AF.Identity, scale=svec[1]), "sS")

        def c1_s3(j):
            st = chain1_state
            b.wait("scalar", "sS", st["s"])
            st["s"] = b.op("scalar", lambda e: e.activation(
                out=sg2, in_=gq, func=AF.Sigmoid), "sS")

        def c1_s4v(j):
            st = chain1_state
            cb = j % 2
            b.wait("scalar", "sS", st["s"])
            b.wait("scalar", "sV", st["u16_v"])
            sv = b.op("scalar", lambda e: e.activation(
                out=i32u, in_=sg2, func=AF.Identity, scale=65536.0), "sS")
            b.wait("vector", "sS", sv)
            vv = b.op("vector", lambda e: e.tensor_scalar(
                out=sgq, in0=i32u, scalar1=65535.0, scalar2=1.0 / 65536.0,
                op0=ALU.min, op1=ALU.mult), "sV")
            st["u16_v"] = vv
            b.wait("vector", "sV", vv)
            vv = b.op("vector", lambda e, cb=cb: e.tensor_tensor(
                out=gchunk[:, cb, :], in0=gq, in1=sgq, op=ALU.mult), "sV")
            b.wait("vector", "sV", vv)
            st["p2r_last"][cb] = b.op("vector", lambda e, j=j, cb=cb: e.tensor_reduce(
                out=gmaxc[:, j:j + 1], in_=gchunk[:, cb, :], axis=AX.X,
                op=ALU.max, apply_absolute_value=True), "sV")
            b.wait("sync", "sV", vv)
            b.dma("sync", gate_d[j], gchunk[:, cb, :], "qS")
            st["p2_last"][cb] = hand()

        def chain1_pump(i, th):
            if th == 0:
                c1_load(i)
                if i >= 1:
                    c1_s1(i - 1)
            elif th == 1:
                if i >= 1:
                    c1_s2(i - 1)
            elif th == 2:
                if i >= 1:
                    c1_s3(i - 1)
            else:
                if i >= 1:
                    c1_s4v(i - 1)

        def mm_updown(dst_d, maxc, wready, interleave=None):
            b.wait("tensor", "sQ", wready)
            b.wait("tensor", "sV", xq_ready)
            b.barrier("tensor", ["sS", "sV", "sGp"])
            b.barrier("gpsimd", ["sS"])
            pfree2 = [0, 0, 0, 0]
            pe2 = [None] * 4
            gl = [0, 0, 0, 0]
            sj = 0
            gh_tick = [0, 0, 0, 0]
            for i in range(24):
                for th in range(4):
                    sl = sj % 4
                    if pe2[sl] is not None:
                        b.wait("tensor", pe2[sl], pfree2[sl])
                    for c in range(8):
                        mv = b.op("tensor", lambda e, i=i, th=th, c=c, sl=sl: e.matmul(
                            out=pslot[sl], lhsT=wT[:, c, i * 128:(i + 1) * 128],
                            rhs=xqT[:, c, th * 512:(th + 1) * 512],
                            start=(c == 0), stop=(c == 7)), "sT" if c == 7 else None)
                    gb = sj % 4
                    b.wait("scalar", "sT", mv)
                    b.wait("scalar", "sQg", gh_tick[gb])
                    b.wait("scalar", "sV", gl[gb])
                    ev = b.op("scalar", lambda e, sl=sl, gb=gb: e.activation(
                        out=gstg[:, gb, :], in_=pslot[sl], func=AF.Identity), "sS")
                    pfree2[sl], pe2[sl] = ev, "sS"
                    b.wait("vector", "sS", ev)
                    gl[gb] = b.op("vector", lambda e, gb=gb, col=sj, maxc=maxc:
                                  e.tensor_reduce(
                        out=maxc[:, col:col + 1], in_=gstg[:, gb, :], axis=AX.X,
                        op=ALU.max, apply_absolute_value=True), "sV")
                    b.wait("gpsimd", "sS", ev)
                    b.dma("gpsimd", dst_d[i, :, th * 512:(th + 1) * 512],
                          gstg[:, gb, :], "qG")
                    if gb == 3:
                        hv = ghand()
                        gh_tick[0] = gh_tick[1] = gh_tick[2] = gh_tick[3] = hv
                    sj += 1
                    if interleave is not None:
                        interleave(i, th)
            return b.op("vector", lambda e: e.nop(), "sV")

        wready = load_wT(0)
        gdone = mm_updown(gate_d, gmaxc, wready)
        chain1_state["gate_store_done"] = b.cnt["sQg"]
        b.wait("vector", "sV", gdone)
        vv = b.op("vector", lambda e: e.tensor_reduce(
            out=gq[:, 0:1], in_=gmaxc, axis=AX.X, op=ALU.max), "sV")
        b.wait("vector", "sV", vv)
        fold_max(gq[:, 0:1], 1)
        allreduce_scales(1, [(1, 1)])

        b.barrier("sync", ["sT"])
        wready = load_wT(1)
        udone = mm_updown(up_d, umaxc, wready, interleave=chain1_pump)
        b.barrier("sync", ["sV", "sS", "sT"])
        c1_s1(23)
        c1_s2(23)
        c1_s3(23)
        c1_s4v(23)
        up_store_done = b.cnt["qS"]

        b.barrier("vector", ["sV", "sS"])
        vv = b.op("vector", lambda e: e.tensor_reduce(
            out=gq[:, 0:1], in_=umaxc, axis=AX.X, op=ALU.max), "sV")
        b.wait("vector", "sV", vv)
        fold_max(gq[:, 0:1], 2)
        vv = b.op("vector", lambda e: e.tensor_reduce(
            out=gq[:, 0:1], in_=gmaxc[:, 0:24], axis=AX.X, op=ALU.max), "sV")
        b.wait("vector", "sV", vv)
        fold_max(gq[:, 0:1], 3)
        allreduce_scales(2, [(2, 2), (3, 3)])

        wdT = wT.rearrange("p c o -> p (c o)")[:, 0:24 * 1024].rearrange(
            "p (i o) -> p i o", i=24)
        wd_view_g = [wq_g[R, 2].rearrange("c p o -> (c p o)").rearrange(
            "(i p o) -> i p o", i=24, p=128) for R in range(8)]

        s23 = af.alloc((1,))
        vv = b.op("vector", lambda e: e.tensor_scalar(
            out=s23, in0=svec[3], scalar1=svec[2], scalar2=None, op0=ALU.mult), "sV")
        b.wait("scalar", "sV", vv)
        b.wait("sync", "qS", up_store_done)
        upst_tick = hand()
        b.barrier("sync", ["sV", "sS"])
        b.wait("gpsimd", "sQ", upst_tick)
        b.barrier("gpsimd", ["sT", "sV", "sS"])
        gq2 = [gq, af.alloc((2048,))]
        Ubuf = [xqT[:, j, :] for j in range(4)]
        Pbuf = [xqT[:, 4 + j, :] for j in range(4)]
        Obuf = [gchunk[:, j, :] for j in range(4)]
        u_read = [0, 0, 0, 0]
        p_read = [0, 0, 0, 0]
        g_tt = [0, 0]
        o_store = [0, 0, 0, 0]
        utick = [0] * 24
        ptick = [0] * 24
        prev_tt = 0
        prev_s2 = 0
        p3r_last = 0

        pend_load = []
        pend_store = []

        def load_pair(j):
            sl = j % 4
            b.wait("gpsimd", "sV", u_read[sl])
            b.dma("gpsimd", Ubuf[sl], up_d[j], "qG")
            b.wait("gpsimd", "sS", p_read[sl])
            b.dma("gpsimd", Pbuf[sl], gate_d[j], "qG")
            pend_load.append(j)

        def tick_all():
            if pend_load:
                gt = ghand()
                for j in pend_load:
                    utick[j] = ptick[j] = gt
                pend_load.clear()
            if pend_store:
                pt = hand()
                for si in pend_store:
                    o_store[si % 4] = pt
                pend_store.clear()

        for j in range(4):
            load_pair(j)
        tick_all()
        wd_pairs = [(R, wi) for R in range(8) for wi in range(24)]
        for i in range(24):
            sl = i % 4
            cb = i % 2
            cbo = i % 4
            b.wait("scalar", "sQg", ptick[i])
            b.wait("scalar", "sS", prev_s2)
            s1 = b.op("scalar", lambda e, sl=sl: e.activation(
                out=i32t, in_=Pbuf[sl], func=AF.Identity, scale=rvec[3]), "sS")
            p_read[sl] = s1
            b.wait("scalar", "sS", s1)
            b.wait("scalar", "sV", g_tt[cb])
            prev_s2 = b.op("scalar", lambda e, cb=cb: e.activation(
                out=gq2[cb], in_=i32t, func=AF.Identity, scale=s23), "sS")
            b.wait("vector", "sQg", utick[i])
            b.wait("vector", "sV", prev_tt)
            v1 = b.op("vector", lambda e, sl=sl: e.tensor_scalar(
                out=i32u, in0=Ubuf[sl], scalar1=rvec[2], scalar2=None,
                op0=ALU.mult), "sV")
            u_read[sl] = v1
            b.wait("vector", "sV", v1)
            b.wait("vector", "sS", prev_s2)
            b.wait("vector", "sQ", o_store[cbo])
            tt = b.op("vector", lambda e, cb=cb, cbo=cbo: e.tensor_tensor(
                out=Obuf[cbo], in0=i32u, in1=gq2[cb], op=ALU.mult), "sV")
            g_tt[cb] = tt
            prev_tt = tt
            b.wait("vector", "sV", tt)
            p3r_last = b.op("vector", lambda e, i=i, cbo=cbo: e.tensor_reduce(
                out=umaxc[:, 24 + i:25 + i], in_=Obuf[cbo], axis=AX.X,
                op=ALU.max, apply_absolute_value=True), "sV")
            b.wait("sync", "sV", tt)
            b.dma("sync", up_d[i], Obuf[cbo], "qS")
            pend_store.append(i)
            if i + 4 < 24:
                load_pair(i + 4)
            if i == 0:
                b.barrier("sync", ["sT"])
            for Ri in wd_pairs[i * 8:(i + 1) * 8]:
                R, wi = Ri
                b.dma("sync", wdT[:, wi, R * 128:(R + 1) * 128],
                      wd_view_g[R][wi].rearrange("p o -> p o"), "qS")
            if i % 2 == 1:
                tick_all()
        tick_all()
        wd_ready = hand()

        b.wait("vector", "sV", p3r_last)
        vv = b.op("vector", lambda e: e.tensor_reduce(
            out=gq[:, 0:1], in_=umaxc[:, 24:48], axis=AX.X, op=ALU.max), "sV")
        b.wait("vector", "sV", vv)
        fold_max(gq[:, 0:1], 4)
        allreduce_scales(3, [(4, 4)])


        dslot = [psV[:, 0:512], psV[:, 512:1024], psA[:, 0:512], psA[:, 512:1024]]
        dfree = [0, 0, 0, 0]
        dj = 0
        ocol_last = [0, 0]
        oq_ready = [0, 0]
        out_st = 0
        b.barrier("tensor", ["sS", "sV", "sGp"])

        def emit_ocol(kt_i):
            ob = kt_i % 2
            b.wait("sync", "sT", ocol_last[ob])
            b.dma("sync", ocol[:, ob, :, :],
                  up_d[:, :, kt_i * 128:(kt_i + 1) * 128].rearrange(
                      "i p t -> p i t"), "qS")
            lv = hand()
            b.wait("vector", "sQ", lv)
            b.wait("vector", "sV", max(oq_ready[0], oq_ready[1]))
            vv = b.op("vector", lambda e, ob=ob: e.tensor_scalar(
                out=i32t, in0=ocol[:, ob, 0:16, :].rearrange("p i t -> p (i t)"),
                scalar1=rvec[4], scalar2=None, op0=ALU.mult), "sV")
            b.wait("vector", "sV", vv)
            vv = b.op("vector", lambda e, ob=ob: e.tensor_scalar(
                out=ocol[:, ob, 0:16, :].rearrange("p i t -> p (i t)"), in0=i32t,
                scalar1=svec[4], scalar2=None, op0=ALU.mult), "sV")
            b.wait("vector", "sV", vv)
            vv = b.op("vector", lambda e, ob=ob: e.tensor_scalar(
                out=i32t[:, 0:1024], in0=ocol[:, ob, 16:24, :].rearrange("p i t -> p (i t)"),
                scalar1=rvec[4], scalar2=None, op0=ALU.mult), "sV")
            b.wait("vector", "sV", vv)
            oq_ready[ob] = b.op("vector", lambda e, ob=ob: e.tensor_scalar(
                out=ocol[:, ob, 16:24, :].rearrange("p i t -> p (i t)"), in0=i32t[:, 0:1024],
                scalar1=svec[4], scalar2=None, op0=ALU.mult), "sV")

        b.barrier("vector", ["sS"])
        emit_ocol(0)
        for kt_i in range(NT):
            ob = kt_i % 2
            b.wait("tensor", "sV", oq_ready[ob])
            b.wait("tensor", "sQ", wd_ready)
            for h in range(2):
                sl = dj % 4
                b.wait("tensor", "sS", dfree[sl])
                for i in range(24):
                    mv = b.op("tensor", lambda e, h=h, i=i, sl=sl, ob=ob: e.matmul(
                        out=dslot[sl], lhsT=ocol[:, ob, i, :],
                        rhs=wdT[:, i, h * 512:(h + 1) * 512],
                        start=(i == 0), stop=(i == 23)), "sT" if i == 23 else None)
                b.wait("scalar", "sT", mv)
                ev = b.op("scalar", lambda e, sl=sl, h=h: e.activation(
                    out=oout[:, h * 512:(h + 1) * 512], in_=dslot[sl],
                    func=AF.Identity), "sS")
                dfree[sl] = ev
                dj += 1
            ocol_last[ob] = b.cnt["sT"]
            if kt_i + 1 < NT:
                emit_ocol(kt_i + 1)
            b.wait("sync", "sQ", out_st)
            b.dma("sync", xffn, x1_d[kt_i], "qS")
            lv = hand()
            b.wait("vector", "sS", dfree[(dj - 1) % 4])
            sv = ln_apply(oout, oout)
            b.wait("vector", "sS", sv)
            b.wait("vector", "sQ", lv)
            vv = b.op("vector", lambda e: e.tensor_tensor(
                out=xffn, in0=xffn, in1=oout, op=ALU.add), "sV")
            b.wait("sync", "sV", vv)
            b.dma("sync", out_d[kt_i], xffn, "qS")
            out_st = hand()
            b.wait("scalar", "sV", vv)

        b.wait("sync", "qS", b.cnt["qS"])
        af.pop(); ab.pop(); ai.pop()

        blk = es.enter_context(nc.Block())

        @blk.sync
        def _(e):
            for f in b.prog["sync"]:
                f(e)

        @blk.tensor
        def _(e):
            for f in b.prog["tensor"]:
                f(e)

        @blk.vector
        def _(e):
            for f in b.prog["vector"]:
                f(e)

        @blk.scalar
        def _(e):
            for f in b.prog["scalar"]:
                f(e)

        @blk.gpsimd
        def _(e):
            for f in b.prog["gpsimd"]:
                f(e)

    return nc


_NC_CACHE = None


def kernel(L0_attn_x, L0_attn_W, L0_attn_b, L0_ffn_w_gate, L0_ffn_w_up, L0_ffn_w_down):
    global _NC_CACHE
    x = np.ascontiguousarray(np.asarray(L0_attn_x, dtype=np.float32)).reshape(16384, D)
    W = np.ascontiguousarray(np.asarray(L0_attn_W, dtype=np.float32))
    bb = np.asarray(L0_attn_b, dtype=np.float32)
    wg = np.ascontiguousarray(np.asarray(L0_ffn_w_gate, dtype=np.float32))
    wu = np.ascontiguousarray(np.asarray(L0_ffn_w_up, dtype=np.float32))
    wd = np.ascontiguousarray(np.asarray(L0_ffn_w_down, dtype=np.float32))

    xt = x.reshape(128, 128, D)
    b_q = np.ascontiguousarray(bb[0:1024].reshape(8, 128).T)
    b_k = np.ascontiguousarray(bb[1024:2048].reshape(8, 128).T)
    b_v = np.ascontiguousarray(bb[2048:3072].reshape(1, 1024))
    identb = np.eye(128, dtype=ml_dtypes.bfloat16)
    identf = np.eye(128, dtype=np.float32)
    onesf = np.ones((1, 128), dtype=np.float32)

    in_maps = []
    for r in range(8):
        q = np.arange(128)[:, None]
        kk = np.arange(1024)[None, :]
        m = (kk <= 128 * r + q).astype(np.float32)
        mneg = (m - 1.0) * 1e30
        in_maps.append({
            "x_own": np.ascontiguousarray(xt[r::8]),
            "W": W,
            "b_q": b_q, "b_k": b_k, "b_v": b_v,
            "wg_sl": np.ascontiguousarray(
                wg[384 * r:384 * (r + 1)].reshape(3, 128, D).transpose(1, 0, 2)),
            "wu_sl": np.ascontiguousarray(
                wu[384 * r:384 * (r + 1)].reshape(3, 128, D).transpose(1, 0, 2)),
            "wd_sl": np.ascontiguousarray(wd[128 * r:128 * (r + 1)]),
            "mask1": m, "maskneg": mneg,
            "ident_bf": identb, "ident_f": identf, "ones_f": onesf,
        })
    if _NC_CACHE is None:
        _NC_CACHE = build()
    res = run_bass_kernel_spmd(_NC_CACHE, in_maps, core_ids=list(range(8)))
    global _LAST_RES
    _LAST_RES = res
    y = np.zeros((128, 128, D), dtype=np.float32)
    for r in range(8):
        y[r::8] = res.results[r]["out_own"]
    return y.reshape(2, 8192, D)


# version 20 (final)
# speedup vs baseline: 1.1563x; 1.0167x over previous
import contextlib
import numpy as np
import ml_dtypes
import concourse.bass as bass
import concourse.mybir as mybir
from concourse.bass_utils import run_bass_kernel_spmd

F32 = mybir.dt.float32
BF16 = mybir.dt.bfloat16
I32 = mybir.dt.int32
AF = mybir.ActivationFunctionType
ALU = mybir.AluOpType
AX = mybir.AxisListType
RG = [[0, 1, 2, 3, 4, 5, 6, 7]]

QMAX = 32767.0
WQMAX = 7.0
NT = 16
D = 1024
I3 = 3072
EPS = 1e-5
SSCALE = 0.03125


class Bld:
    def __init__(self, nc):
        self.nc = nc
        self.prog = {e: [] for e in ("sync", "tensor", "vector", "scalar", "gpsimd")}
        self.cnt = {}
        self.sems = {}

    def sem(self, name, es):
        self.sems[name] = es.enter_context(self.nc.semaphore(name))
        self.cnt[name] = 0

    def emit(self, eng, fn):
        self.prog[eng].append(fn)

    def wait(self, eng, sem, val):
        if val is None or val <= 0:
            return
        h = self.sems[sem]
        self.emit(eng, lambda e, h=h, v=val: e.wait_ge(h, v))

    def barrier(self, eng, sems):
        for s in sems:
            self.wait(eng, s, self.cnt[s])

    def dma(self, eng, out, in_, sem):
        h = self.sems[sem]
        self.cnt[sem] += 16
        self.emit(eng, lambda e, o=out, i=in_, h=h: e.dma_start(out=o, in_=i).then_inc(h, 16))
        return self.cnt[sem]

    def op(self, eng, fn, sem=None):
        if sem is None:
            self.emit(eng, fn)
            return None
        h = self.sems[sem]
        self.cnt[sem] += 1
        self.emit(eng, lambda e, f=fn, h=h: f(e).then_inc(h, 1))
        return self.cnt[sem]


class Arena:
    def __init__(self, nc, name, n, dtype, es):
        self.t = es.enter_context(nc.sbuf_tensor(name, [128, n], dtype))
        self.n = n
        self.off = 0
        self.marks = []

    def alloc(self, shape):
        sz = int(np.prod(shape))
        assert self.off + sz <= self.n, f"arena {self.t.name} overflow {self.off}+{sz}>{self.n}"
        ap = self.t[:, self.off:self.off + sz]
        if len(shape) > 1:
            letters = "abcdefg"[:len(shape)]
            pat = "p (" + " ".join(letters) + ") -> p " + " ".join(letters)
            ap = ap.rearrange(pat, **{letters[i]: shape[i] for i in range(len(shape) - 1)})
        self.off += sz
        return ap

    def push(self):
        self.marks.append(self.off)

    def pop(self):
        self.off = self.marks.pop()


def build():
    nc = bass.Bass("TRN2", target_bir_lowering=False)

    x_in = nc.dram_tensor("x_own", [NT, 128, D], F32, kind="ExternalInput").ap()
    W_in = nc.dram_tensor("W", [D, 3 * D], F32, kind="ExternalInput").ap()
    bq_in = nc.dram_tensor("b_q", [128, 8], F32, kind="ExternalInput").ap()
    bk_in = nc.dram_tensor("b_k", [128, 8], F32, kind="ExternalInput").ap()
    bv_in = nc.dram_tensor("b_v", [1, D], F32, kind="ExternalInput").ap()
    wg_in = nc.dram_tensor("wg_sl", [128, 3, D], F32, kind="ExternalInput").ap()
    wu_in = nc.dram_tensor("wu_sl", [128, 3, D], F32, kind="ExternalInput").ap()
    wd_in = nc.dram_tensor("wd_sl", [128, I3], F32, kind="ExternalInput").ap()
    mk_in = nc.dram_tensor("mask1", [128, 1024], F32, kind="ExternalInput").ap()
    mkn_in = nc.dram_tensor("maskneg", [128, 1024], F32, kind="ExternalInput").ap()
    idb_in = nc.dram_tensor("ident_bf", [128, 128], BF16, kind="ExternalInput").ap()
    idf_in = nc.dram_tensor("ident_f", [128, 128], F32, kind="ExternalInput").ap()
    ones_in = nc.dram_tensor("ones_f", [1, 128], F32, kind="ExternalInput").ap()
    out_d = nc.dram_tensor("out_own", [NT, 128, D], F32, kind="ExternalOutput").ap()


    kt_own2 = [nc.dram_tensor(f"kt_own{bi}", [8, 128, 8, 128], BF16, kind="Internal").ap()
               for bi in range(2)]
    kt_g2 = [nc.dram_tensor(f"kt_g{bi}", [8, 8, 128, 8, 128], BF16, kind="Internal",
                            addr_space="Shared").ap()
             for bi in range(2)]
    v_own2 = [nc.dram_tensor(f"v_own{bi}", [8, 128, D], BF16, kind="Internal").ap()
              for bi in range(2)]
    v_g2 = [nc.dram_tensor(f"v_g{bi}", [8, 8, 128, D], BF16, kind="Internal",
                           addr_space="Shared").ap()
            for bi in range(2)]
    wq_own = nc.dram_tensor("wq_own", [3, 8, 128, 384], BF16, kind="Internal").ap()
    wq_g = nc.dram_tensor("wq_g", [8, 3, 8, 128, 384], BF16, kind="Internal",
                          addr_space="Shared").ap()
    xlnT_d = nc.dram_tensor("xlnT_d", [8, 128, NT * 128], BF16, kind="Internal").ap()
    x1_d = nc.dram_tensor("x1_d", [NT, 128, D], F32, kind="Internal").ap()
    gate_d = nc.dram_tensor("gate_d", [24, 128, NT * 128], BF16, kind="Internal").ap()
    up_d = nc.dram_tensor("up_d", [24, 128, NT * 128], BF16, kind="Internal").ap()
    cc_in = [nc.dram_tensor(f"cc_in{i}", [1, 64], F32, kind="Internal").ap() for i in range(4)]
    cc_out = [nc.dram_tensor(f"cc_out{i}", [1, 64], F32, kind="Internal",
                             addr_space="Shared").ap() for i in range(4)]

    es = contextlib.ExitStack()
    with es:
        af = Arena(nc, "af", 17400, F32, es)
        ab = Arena(nc, "ab", 60544, BF16, es)
        ai = Arena(nc, "ai", 4096, I32, es)

        masks = af.alloc((1024,))
        masksn = af.alloc((1024,))
        bvb = af.alloc((D,))
        bq = af.alloc((8,))
        bk = af.alloc((8,))
        identf = af.alloc((128,))
        onesf = af.alloc((128,))
        maxcols = af.alloc((NT,))
        ccs = af.alloc((64,))
        ccr = af.alloc((64,))
        svec = [af.alloc((1,)) for _ in range(5)]
        rvec = [af.alloc((1,)) for _ in range(5)]
        ln = {k: af.alloc((1,)) for k in
              ("sum", "ssq", "mean", "msq", "var", "std", "rs", "nms")}
        lnx = [af.alloc((1,)) for _ in range(2)]
        sqd = af.alloc((1024,))
        identb = ab.alloc((128,))

        psA = es.enter_context(nc.psum_tensor("psA", [128, 1024], F32))
        psB = es.enter_context(nc.psum_tensor("psB", [128, 1024], F32))
        psV = es.enter_context(nc.psum_tensor("psV", [128, 1024], F32))
        psT = es.enter_context(nc.psum_tensor("psT", [128, 1024], BF16))
        psM = es.enter_context(nc.psum_tensor("psM", [128, 512], F32))

        b = Bld(nc)
        for s in ("qS", "qG", "sT", "sV", "sS", "sGp", "qKV0", "qKV1", "qW", "sQ",
                  "qC", "sQg"):
            b.sem(s, es)

        def hand():
            hq = b.sems["qS"]
            vq = b.cnt["qS"]
            b.emit("sync", lambda e, h=hq, v=vq: e.wait_ge(h, v))
            hs = b.sems["sQ"]
            b.cnt["sQ"] += 1
            b.emit("sync", lambda e, h=hs: e.nop().then_inc(h, 1))
            return b.cnt["sQ"]

        def ghand():
            hq = b.sems["qG"]
            vq = b.cnt["qG"]
            b.emit("gpsimd", lambda e, h=hq, v=vq: e.wait_ge(h, v))
            hs = b.sems["sQg"]
            b.cnt["sQg"] += 1
            b.emit("gpsimd", lambda e, h=hs: e.nop().then_inc(h, 1))
            return b.cnt["sQg"]

        def ln_apply(tile_f32, out_ap):
            b.barrier("scalar", ["sV", "sS"])
            b.barrier("vector", ["sS", "sV"])
            vs = b.op("scalar", lambda e: e.activation(
                out=sqd, in_=tile_f32, func=AF.Square, accum_out=ln["ssq"]), "sS")
            vv = b.op("vector", lambda e: e.tensor_reduce(
                out=ln["sum"], in_=tile_f32, axis=AX.X, op=ALU.add), "sV")
            b.wait("vector", "sS", vs)
            b.wait("vector", "sV", vv)
            b.op("vector", lambda e: e.tensor_scalar(
                out=ln["mean"], in0=ln["sum"], scalar1=1.0 / D, scalar2=None, op0=ALU.mult))
            vv = b.op("vector", lambda e: e.tensor_scalar(
                out=ln["msq"], in0=ln["ssq"], scalar1=1.0 / D, scalar2=None, op0=ALU.mult), "sV")
            b.wait("vector", "sV", vv)
            vv = b.op("vector", lambda e: e.tensor_scalar(
                out=ln["var"], in0=ln["mean"], scalar1=ln["mean"], scalar2=None, op0=ALU.mult), "sV")
            b.wait("vector", "sV", vv)
            vv = b.op("vector", lambda e: e.scalar_tensor_tensor(
                out=ln["var"], in0=ln["msq"], scalar=EPS, in1=ln["var"],
                op0=ALU.add, op1=ALU.subtract), "sV")
            b.wait("vector", "sV", vv)
            b.wait("scalar", "sV", vv)
            vs = b.op("scalar", lambda e: e.activation(
                out=ln["std"], in_=ln["var"], func=AF.Sqrt), "sS")
            b.wait("vector", "sS", vs)
            vv = b.op("vector", lambda e: e.reciprocal(out=ln["rs"], in_=ln["std"]), "sV")
            b.wait("vector", "sV", vv)
            vv = b.op("vector", lambda e: e.tensor_scalar(
                out=ln["nms"], in0=ln["mean"], scalar1=ln["rs"], scalar2=-1.0,
                op0=ALU.mult, op1=ALU.mult), "sV")
            b.wait("scalar", "sV", vv)
            return b.op("scalar", lambda e, o=out_ap, t=tile_f32: e.activation(
                out=o, in_=t, func=AF.Identity, scale=ln["rs"], bias=ln["nms"]), "sS")

        def ln_apply_recl(tile_f32, out_ap, recl_ap, lrun_col):
            b.barrier("scalar", ["sV", "sS"])
            b.barrier("vector", ["sS", "sV"])
            rr = b.op("vector", lambda e: e.reciprocal(
                out=recl_ap, in_=lrun_col), "sV")
            vs = b.op("scalar", lambda e: e.activation(
                out=sqd, in_=tile_f32, func=AF.Square, accum_out=ln["ssq"]), "sS")
            vv = b.op("vector", lambda e: e.tensor_reduce(
                out=ln["sum"], in_=tile_f32, axis=AX.X, op=ALU.add), "sV")
            b.wait("vector", "sS", vs)
            b.wait("vector", "sV", max(vv, rr))
            b.op("vector", lambda e: e.tensor_scalar(
                out=ln["mean"], in0=ln["sum"], scalar1=recl_ap, scalar2=1.0 / D,
                op0=ALU.mult, op1=ALU.mult))
            vv = b.op("vector", lambda e: e.tensor_scalar(
                out=lnx[0], in0=recl_ap, scalar1=recl_ap, scalar2=1.0 / D,
                op0=ALU.mult, op1=ALU.mult), "sV")
            b.wait("vector", "sV", vv)
            vv = b.op("vector", lambda e: e.tensor_scalar(
                out=ln["msq"], in0=ln["ssq"], scalar1=lnx[0], scalar2=None,
                op0=ALU.mult), "sV")
            b.wait("vector", "sV", vv)
            vv = b.op("vector", lambda e: e.tensor_scalar(
                out=ln["var"], in0=ln["mean"], scalar1=ln["mean"], scalar2=None, op0=ALU.mult), "sV")
            b.wait("vector", "sV", vv)
            vv = b.op("vector", lambda e: e.scalar_tensor_tensor(
                out=ln["var"], in0=ln["msq"], scalar=EPS, in1=ln["var"],
                op0=ALU.add, op1=ALU.subtract), "sV")
            b.wait("vector", "sV", vv)
            b.wait("scalar", "sV", vv)
            vs = b.op("scalar", lambda e: e.activation(
                out=ln["std"], in_=ln["var"], func=AF.Sqrt), "sS")
            b.wait("vector", "sS", vs)
            vv = b.op("vector", lambda e: e.reciprocal(out=ln["rs"], in_=ln["std"]), "sV")
            b.wait("vector", "sV", vv)
            b.op("vector", lambda e: e.tensor_scalar(
                out=ln["nms"], in0=ln["mean"], scalar1=ln["rs"], scalar2=-1.0,
                op0=ALU.mult, op1=ALU.mult))
            vv = b.op("vector", lambda e: e.tensor_scalar(
                out=lnx[1], in0=ln["rs"], scalar1=recl_ap, scalar2=None,
                op0=ALU.mult), "sV")
            b.wait("scalar", "sV", vv)
            return b.op("scalar", lambda e, o=out_ap, t=tile_f32: e.activation(
                out=o, in_=t, func=AF.Identity, scale=lnx[1], bias=ln["nms"]), "sS")

        def fold_max(src_col, col):
            b.barrier("tensor", ["sV", "sGp"])
            vt = b.op("tensor", lambda e, s=src_col: e.matmul(
                out=psM[0:1, 0:128], lhsT=s, rhs=identf, start=True, stop=True), "sT")
            b.wait("vector", "sT", vt)
            vv = b.op("vector", lambda e, c=col: e.tensor_reduce(
                out=ccs[0:1, c:c + 1], in_=psM[0:1, 0:128], axis=AX.X,
                op=ALU.max, apply_absolute_value=True), "sV")
            b.wait("tensor", "sV", vv)
            return vv

        def allreduce_scales(round_i, cols):
            b.barrier("gpsimd", ["sV"])
            wv = b.dma("gpsimd", cc_in[round_i], ccs[0:1, :], "qG")
            b.wait("gpsimd", "qG", wv)
            cv = b.op("gpsimd", lambda e, i=round_i: e.collective_compute(
                "AllReduce", ALU.max, RG, ins=[cc_in[i].opt()], outs=[cc_out[i].opt()]), "qC")
            b.wait("gpsimd", "qC", cv)
            rv = b.dma("gpsimd", ccr[0:1, :], cc_out[round_i], "qG")
            b.wait("tensor", "qG", rv)
            for col, idx in cols:
                vt = b.op("tensor", lambda e, c=col: e.matmul(
                    out=psM[:, 0:1], lhsT=onesf[0:1, :], rhs=ccr[0:1, c:c + 1],
                    start=True, stop=True), "sT")
                b.wait("vector", "sT", vt)
                vv = b.op("vector", lambda e, i=idx: e.tensor_scalar(
                    out=svec[i], in0=psM[:, 0:1], scalar1=1.0 / QMAX, scalar2=1e-12,
                    op0=ALU.mult, op1=ALU.max), "sV")
                b.wait("vector", "sV", vv)
                vv = b.op("vector", lambda e, i=idx: e.reciprocal(
                    out=rvec[i], in_=svec[i]), "sV")
                b.wait("tensor", "sV", vv)
            vv = b.cnt["sV"]
            for eng in ("vector", "gpsimd", "scalar"):
                b.wait(eng, "sV", vv)
            return vv

        b.dma("sync", masks, mk_in, "qS")
        b.dma("sync", masksn, mkn_in, "qS")
        b.dma("sync", bq, bq_in, "qS")
        b.dma("sync", bk, bk_in, "qS")
        b.dma("sync", identf, idf_in, "qS")
        b.dma("sync", onesf[0:1, :], ones_in, "qS")
        b.dma("sync", identb, idb_in, "qS")
        b.op("vector", lambda e: e.memset(ccs[0:1, :], 0.0), "sV")

        af.push()
        bvrow = af.alloc((D,))

        b.dma("sync", bvrow[0:1, :], bv_in, "qS")
        cv0 = hand()
        b.wait("tensor", "sQ", cv0)
        for h in range(2):
            vt = b.op("tensor", lambda e, h=h: e.matmul(
                out=psM[:, 0:512], lhsT=onesf[0:1, :], rhs=bvrow[0:1, 512 * h:512 * (h + 1)],
                start=True, stop=True), "sT")
            b.wait("vector", "sT", vt)
            vv = b.op("vector", lambda e, h=h: e.tensor_copy(
                out=bvb[:, 512 * h:512 * (h + 1)], in_=psM[:, 0:512]), "sV")
            b.wait("tensor", "sV", vv)
        af.pop()

        af.push(); ab.push(); ai.push()
        qT = ab.alloc((8, 2048))

        af.push(); ab.push()
        b.barrier("sync", ["sT", "sV", "sS"])
        W_sb = ab.alloc((8, 3 * D))
        xlnT = ab.alloc((8, 2048))
        xlnb = ab.alloc((D,))
        kvst = ab.alloc((4, 512))
        Wstg = af.alloc((2, 3 * D))
        xst = af.alloc((2, D))

        last_cast = [0, 0]
        for c in range(8):
            buf = c % 2
            if buf == 0:
                b.wait("sync", "sV", last_cast[buf])
                b.dma("sync", Wstg[:, buf, :], W_in[c * 128:(c + 1) * 128, :], "qS")
                tick, tsem = hand(), "sQ"
            else:
                b.wait("gpsimd", "sV", last_cast[buf])
                b.dma("gpsimd", Wstg[:, buf, :], W_in[c * 128:(c + 1) * 128, :], "qG")
                tick, tsem = ghand(), "sQg"
            b.wait("vector", tsem, tick)
            last_cast[buf] = b.op("vector", lambda e, c=c, buf=buf: e.tensor_copy(
                out=W_sb[:, c, :], in_=Wstg[:, buf, :]), "sV")

        xst_free = [0, 0]
        for k in range(NT):
            buf = k % 2
            b.wait("sync", "sS", xst_free[buf])
            b.dma("sync", xst[:, buf, :], x_in[k], "qS")
            lv = hand()
            b.wait("scalar", "sQ", lv)
            b.wait("vector", "sQ", lv)
            b.barrier("scalar", ["sT"])
            sv = ln_apply(xst[:, buf, :], xlnb)
            xst_free[buf] = sv
            b.wait("tensor", "sS", sv)
            b.barrier("tensor", ["sV"])
            for c in range(8):
                vt = b.op("tensor", lambda e, c=c: e.transpose(
                    out=psT[:, c * 128:(c + 1) * 128],
                    in_=xlnb[:, c * 128:(c + 1) * 128], identity=identb),
                    "sT" if c == 7 else None)
            b.wait("vector", "sT", vt)
            vv = b.op("vector", lambda e, k=k: e.tensor_copy(
                out=xlnT[:, :, k * 128:(k + 1) * 128],
                in_=psT[:].rearrange("p (c t) -> p c t", c=8)), "sV")
            b.wait("tensor", "sV", vv)

        pslot = [psA[:, 0:512], psA[:, 512:1024], psB[:, 0:512], psB[:, 512:1024]]
        pfree = [0, 0, 0, 0]
        pfeng = [None] * 4
        si = 0
        kst_hand = [0, 0, 0, 0]
        gather_done = {}
        hv = hand()
        b.wait("tensor", "sQ", hv)
        b.barrier("tensor", ["sV"])
        for th in range(4):
            for j in range(16):
                sl = si % 4
                if pfeng[sl] is not None:
                    b.wait("tensor", pfeng[sl], pfree[sl])
                for c in range(8):
                    mv = b.op("tensor", lambda e, j=j, th=th, c=c, sl=sl: e.matmul(
                        out=pslot[sl], lhsT=W_sb[:, c, j * 128:(j + 1) * 128],
                        rhs=xlnT[:, c, th * 512:(th + 1) * 512],
                        start=(c == 0), stop=(c == 7)), "sT" if c == 7 else None)
                if j < 8:
                    b.wait("scalar", "sT", mv)
                    ev = b.op("scalar", lambda e, j=j, th=th, sl=sl: e.activation(
                        out=qT[:, j, th * 512:(th + 1) * 512], in_=pslot[sl],
                        func=AF.Identity, bias=bq[:, j:j + 1]), "sS")
                    pfree[sl], pfeng[sl] = ev, "sS"
                else:
                    jc = j - 8
                    kb = si % 4
                    sbuf = kvst[:, kb, :]
                    b.wait("scalar", "sT", mv)
                    b.wait("scalar", "sQ", kst_hand[kb])
                    ev = b.op("scalar", lambda e, jc=jc, sl=sl, sbuf=sbuf: e.activation(
                        out=sbuf, in_=pslot[sl], func=AF.Identity,
                        bias=bk[:, jc:jc + 1]), "sS")
                    pfree[sl], pfeng[sl] = ev, "sS"
                    b.wait("sync", "sS", ev)
                    bi, t4 = th // 2, th % 2
                    b.dma(
                        "sync",
                        kt_own2[bi][t4 * 4:(t4 + 1) * 4, :, jc, :].rearrange(
                            "k p t -> p k t"),
                        sbuf.rearrange("p (k t) -> p k t", k=4), "qS")
                    if kb == 3:
                        hv = hand()
                        kst_hand[0] = kst_hand[1] = kst_hand[2] = kst_hand[3] = hv
                si += 1
            for kt_i in range(th * 4, (th + 1) * 4):
                for h in range(2):
                    sl = si % 4
                    if pfeng[sl] is not None:
                        b.wait("tensor", pfeng[sl], pfree[sl])
                    for c in range(8):
                        mv = b.op("tensor", lambda e, kt_i=kt_i, h=h, c=c, sl=sl: e.matmul(
                            out=pslot[sl],
                            lhsT=xlnT[:, c, kt_i * 128:(kt_i + 1) * 128],
                            rhs=W_sb[:, c, 2048 + h * 512:2048 + (h + 1) * 512],
                            start=(c == 0), stop=(c == 7)), "sT" if c == 7 else None)
                    kb = si % 4
                    sbuf = kvst[:, kb, :]
                    b.wait("vector", "sT", mv)
                    b.wait("vector", "sQ", kst_hand[kb])
                    ev = b.op("vector", lambda e, h=h, sl=sl, sbuf=sbuf: e.tensor_tensor(
                        out=sbuf, in0=pslot[sl], in1=bvb[:, h * 512:(h + 1) * 512],
                        op=ALU.add), "sV")
                    pfree[sl], pfeng[sl] = ev, "sV"
                    b.wait("sync", "sV", ev)
                    bi = kt_i // 8
                    b.dma("sync", v_own2[bi][kt_i % 8, :, h * 512:(h + 1) * 512],
                          sbuf, "qS")
                    if kb == 3:
                        hv = hand()
                        kst_hand[0] = kst_hand[1] = kst_hand[2] = kst_hand[3] = hv
                    si += 1
            if th == 1 or th == 3:
                bi = th // 2
                hv = hand()
                b.wait("gpsimd", "sQ", hv)
                b.op("gpsimd", lambda e, bi=bi: e.collective_compute(
                    "AllGather", ALU.bypass, RG,
                    ins=[kt_own2[bi].opt()], outs=[kt_g2[bi].opt()]), f"qKV{bi}")
                gather_done[bi] = b.op(
                    "gpsimd", lambda e, bi=bi: e.collective_compute(
                        "AllGather", ALU.bypass, RG,
                        ins=[v_own2[bi].opt()], outs=[v_g2[bi].opt()]), f"qKV{bi}")
        af.pop(); ab.pop()

        af.push(); ab.push(); ai.push()
        wsl = af.alloc((3 * D,))
        wabs = af.alloc((3 * D,))
        bmt = af.alloc((3, 1536))
        wq_bf = ab.alloc((3 * D,))
        wtr = ab.alloc((1024,))
        qi32 = ai.alloc((3 * D,))
        b.barrier("sync", ["sT", "sS", "sV"])

        def wqdq(n_free):
            nb = n_free // 32
            b.barrier("vector", ["sT"])
            vv = b.op("vector", lambda e, n=n_free: e.scalar_tensor_tensor(
                out=wabs[:, 0:n], in0=wsl[:, 0:n], scalar=-1.0, in1=wsl[:, 0:n],
                op0=ALU.mult, op1=ALU.max), "sV")
            b.wait("vector", "sV", vv)
            v3 = wabs[:, 0:n_free].rearrange("p (a two c) -> p a two c", a=nb, two=2)
            cur = bmt[:, 0, 0:nb * 16].rearrange("p (a c) -> p a c", a=nb)
            vv = b.op("vector", lambda e, cur=cur, v3=v3: e.tensor_tensor(
                out=cur, in0=v3[:, :, 0, :], in1=v3[:, :, 1, :], op=ALU.max), "sV")
            b.wait("vector", "sV", vv)
            w, buf = 16, 1
            while w > 1:
                nxt = bmt[:, buf, 0:nb * (w // 2)].rearrange("p (a c) -> p a c", a=nb)
                vv = b.op("vector", lambda e, cur=cur, nxt=nxt: e.tensor_tensor(
                    out=nxt, in0=cur[:, :, 0::2], in1=cur[:, :, 1::2], op=ALU.max), "sV")
                b.wait("vector", "sV", vv)
                cur, w, buf = nxt, w // 2, buf ^ 1
            sb_ = bmt[:, 2, 0:nb]
            rb_ = bmt[:, 2, 1536 - nb:1536]
            vv = b.op("vector", lambda e, cur=cur, sb_=sb_: e.tensor_scalar(
                out=sb_, in0=cur.rearrange("p a c -> p (a c)"),
                scalar1=1.0 / WQMAX, scalar2=1e-12, op0=ALU.mult, op1=ALU.max), "sV")
            b.wait("vector", "sV", vv)
            vv = b.op("vector", lambda e, sb_=sb_, rb_=rb_: e.reciprocal(
                out=rb_, in_=sb_), "sV")
            b.wait("vector", "sV", vv)
            vv = b.op("vector", lambda e, nb=nb, rb_=rb_: e.tensor_tensor(
                out=qi32[:, 0:n_free].rearrange("p (a c) -> p a c", a=nb),
                in0=wsl[:, 0:n_free].rearrange("p (a c) -> p a c", a=nb),
                in1=rb_.rearrange("p (a c) -> p a c", a=nb).broadcast_to((128, nb, 32)),
                op=ALU.mult), "sV")
            b.wait("vector", "sV", vv)
            return b.op("vector", lambda e, nb=nb, sb_=sb_: e.tensor_tensor(
                out=wq_bf[:, 0:n_free].rearrange("p (a c) -> p a c", a=nb),
                in0=qi32[:, 0:n_free].rearrange("p (a c) -> p a c", a=nb),
                in1=sb_.rearrange("p (a c) -> p a c", a=nb).broadcast_to((128, nb, 32)),
                op=ALU.mult), "sV")

        last_store = 0
        for widx, src in ((0, wg_in), (1, wu_in)):
            b.barrier("sync", ["sV"])
            b.dma("sync", wsl.rearrange("p (a c) -> p a c", a=3), src, "qS")
            lv = hand()
            b.wait("vector", "sQ", lv)
            qv = wqdq(3 * D)
            b.wait("tensor", "sV", qv)
            for c in range(8):
                for o3 in range(3):
                    vt = b.op("tensor", lambda e, c=c, o3=o3: e.transpose(
                        out=psT[:, o3 * 128:(o3 + 1) * 128],
                        in_=wq_bf[:, (o3 * D + c * 128):(o3 * D + (c + 1) * 128)],
                        identity=identb), "sT" if o3 == 2 else None)
                b.wait("vector", "sT", vt)
                b.wait("vector", "sQ", last_store)
                cvv = b.op("vector", lambda e: e.tensor_copy(
                    out=wtr[:, 0:384], in_=psT[:, 0:384]), "sV")
                b.wait("sync", "sV", cvv)
                b.dma("sync", wq_own[widx, c, :, :], wtr[:, 0:384], "qS")
                last_store = hand()
                b.wait("tensor", "sV", cvv)
        b.barrier("sync", ["sV"])
        b.dma("sync", wsl, wd_in, "qS")
        lv = hand()
        b.wait("vector", "sQ", lv)
        qv = wqdq(I3)
        b.wait("tensor", "sV", qv)
        wd_view = wq_own[2].rearrange("c p o -> (c p o)").rearrange(
            "(i p o) -> i p o", i=24, p=128)
        for i8 in range(3):
            for j in range(8):
                i = i8 * 8 + j
                vt = b.op("tensor", lambda e, i=i, j=j: e.transpose(
                    out=psT[:, j * 128:(j + 1) * 128],
                    in_=wq_bf[:, i * 128:(i + 1) * 128], identity=identb),
                    "sT" if j == 7 else None)
            b.wait("vector", "sT", vt)
            b.wait("vector", "sQ", last_store)
            cvv = b.op("vector", lambda e: e.tensor_copy(out=wtr, in_=psT[:]), "sV")
            b.wait("sync", "sV", cvv)
            b.dma("sync",
                  wd_view[i8 * 8:(i8 + 1) * 8, :, :].rearrange("i p o -> p i o"),
                  wtr.rearrange("p (i o) -> p i o", i=8), "qS")
            last_store = hand()
            b.wait("tensor", "sV", cvv)
        hv = hand()
        b.wait("gpsimd", "sQ", hv)
        wq_gather_done = b.op("gpsimd", lambda e: e.collective_compute(
            "AllGather", ALU.bypass, RG, ins=[wq_own.opt()], outs=[wq_g.opt()]), "qW")
        af.pop(); ab.pop(); ai.pop()

        acc = af.alloc((8, D))
        mrun = af.alloc((8,))
        lrun = af.alloc((8,))
        mnt = af.alloc((1,))
        mneg = af.alloc((1,))
        alpha = af.alloc((1,))
        cmax = af.alloc((1,))
        rowsum = af.alloc((1,))
        recl = af.alloc((1,))
        xst2 = af.alloc((D,))
        aout = af.alloc((D,))
        x1t = af.alloc((D,))
        ktc = ab.alloc((2, 8, 8, 128))
        vc = ab.alloc((2, 8, 1024))
        expb = ab.alloc((1024,))
        attnT = ab.alloc((1024,))
        xlnb2 = ab.alloc((D,))
        xlts2 = ab.alloc((8, 128))

        b.barrier("tensor", ["sS", "sV"])
        b.barrier("sync", ["sT", "sV", "sS"])

        xst2_free = 0
        x1t_store = 0
        xlts2_store = 0
        scoreps = [psA, psB]
        score_free = [0, 0]
        ktc_free = [0, 0]
        vc_free = [0, 0]
        ktc_ready = [0, 0]
        vc_ready = [0, 0]
        last_exp_consumed = 0
        last_attnT_used = 0
        last_acc_upd = 0

        for bat in range(2):
            b.wait("sync", f"qKV{bat}", gather_done[bat])
            visits = [(c, m) for c in range(8) for m in range(c, 8)]
            loads_done = {}
            mvs = {}

            def emit_loads(c):
                cb = c % 2
                b.wait("sync", "sT", ktc_free[cb])
                for r in range(8):
                    b.dma("sync", ktc[:, cb, r, :, :].rearrange("p c t -> p (c t)"),
                          kt_g2[bat][r, c].rearrange("p c t -> p (c t)"), "qS")
                b.wait("sync", "sT", vc_free[cb])
                for r in range(8):
                    b.dma("sync", vc[:, cb, r, :], v_g2[bat][r, c], "qS")
                wv = hand()
                ktc_ready[cb] = wv
                vc_ready[cb] = wv
                loads_done[c] = True

            def emit_A(j):
                c, m = visits[j]
                cb = c % 2
                if c not in loads_done:
                    emit_loads(c)
                sp = scoreps[m % 2]
                k_tile = bat * 8 + m
                b.wait("tensor", "sQ", ktc_ready[cb])
                b.wait("tensor", "sS", score_free[m % 2])
                for h in range(2):
                    for cc in range(8):
                        mv = b.op("tensor", lambda e, h=h, cc=cc, sp=sp, cb=cb, k_tile=k_tile: e.matmul(
                            out=sp[:, h * 512:(h + 1) * 512],
                            lhsT=qT[:, cc, k_tile * 128:(k_tile + 1) * 128],
                            rhs=ktc[:, cb, h * 4:(h + 1) * 4, cc, :],
                            start=(cc == 0), stop=(cc == 7)),
                            "sT" if (cc == 7 and h == 1) else None)
                mvs[j] = mv
                if m == 7:
                    ktc_free[cb] = mv

            def emit_B(j):
                nonlocal last_exp_consumed, last_attnT_used, last_acc_upd
                c, m = visits[j]
                cb = c % 2
                sp = scoreps[m % 2]
                mv = mvs[j]
                b.wait("vector", "sT", mv)
                if m == c:
                    vv = b.op("vector", lambda e, sp=sp: e.tensor_tensor(
                        out=sp[:], in0=sp[:], in1=masks, op=ALU.mult), "sV")
                    b.wait("vector", "sV", vv)
                    vv = b.op("vector", lambda e, sp=sp: e.tensor_tensor(
                        out=sp[:], in0=sp[:], in1=masksn, op=ALU.add), "sV")
                    b.wait("vector", "sV", vv)
                vv = b.op("vector", lambda e, sp=sp: e.tensor_reduce(
                    out=cmax, in_=sp[:], axis=AX.X, op=ALU.max), "sV")
                b.wait("vector", "sV", vv)
                first = (c == 0)
                if first:
                    b.op("vector", lambda e, m=m: e.tensor_copy(
                        out=mrun[:, m:m + 1], in_=cmax))
                    vv = b.op("vector", lambda e: e.tensor_scalar(
                        out=mneg, in0=cmax, scalar1=-SSCALE, scalar2=None,
                        op0=ALU.mult), "sV")
                    b.wait("scalar", "sV", vv)
                else:
                    vv = b.op("vector", lambda e, m=m: e.tensor_tensor(
                        out=mnt, in0=mrun[:, m:m + 1], in1=cmax, op=ALU.max), "sV")
                    b.wait("vector", "sV", vv)
                    vv = b.op("vector", lambda e: e.tensor_scalar(
                        out=mneg, in0=mnt, scalar1=-SSCALE, scalar2=None,
                        op0=ALU.mult), "sV")
                    b.wait("scalar", "sV", vv)
                    av = b.op("scalar", lambda e, m=m: e.activation(
                        out=alpha, in_=mrun[:, m:m + 1], func=AF.Exp,
                        scale=SSCALE, bias=mneg), "sS")
                    b.wait("vector", "sS", av)
                    b.op("vector", lambda e, m=m: e.tensor_copy(
                        out=mrun[:, m:m + 1], in_=mnt))
                b.wait("scalar", "sT", last_exp_consumed)
                sv = b.op("scalar", lambda e, sp=sp: e.activation(
                    out=expb, in_=sp[:], func=AF.Exp, scale=SSCALE, bias=mneg,
                    accum_out=rowsum), "sS")
                score_free[m % 2] = sv
                b.wait("tensor", "sS", sv)
                b.barrier("tensor", ["sV"])
                for r in range(8):
                    tv = b.op("tensor", lambda e, r=r: e.transpose(
                        out=psT[:, r * 128:(r + 1) * 128],
                        in_=expb[:, r * 128:(r + 1) * 128], identity=identb),
                        "sT" if r == 7 else None)
                last_exp_consumed = tv
                b.wait("vector", "sT", tv)
                b.wait("vector", "sT", last_attnT_used)
                cvv = b.op("vector", lambda e: e.tensor_copy(out=attnT, in_=psT[:]), "sV")
                b.wait("tensor", "sV", cvv)
                b.wait("tensor", "sQ", vc_ready[cb])
                b.wait("tensor", "sV", last_acc_upd)
                for r in range(8):
                    for h in range(2):
                        pv = b.op("tensor", lambda e, r=r, h=h, cb=cb: e.matmul(
                            out=psV[:, h * 512:(h + 1) * 512],
                            lhsT=attnT[:, r * 128:(r + 1) * 128],
                            rhs=vc[:, cb, r, h * 512:(h + 1) * 512],
                            start=(r == 0), stop=(r == 7)),
                            "sT" if (r == 7 and h == 1) else None)
                last_attnT_used = pv
                if m == 7:
                    vc_free[cb] = pv
                b.wait("vector", "sT", pv)
                b.wait("vector", "sS", sv)
                if first:
                    b.op("vector", lambda e, m=m: e.tensor_copy(
                        out=lrun[:, m:m + 1], in_=rowsum))
                    last_acc_upd = b.op("vector", lambda e, m=m: e.tensor_copy(
                        out=acc[:, m, :], in_=psV[:]), "sV")
                else:
                    b.op("vector", lambda e, m=m: e.tensor_scalar(
                        out=lrun[:, m:m + 1], in0=lrun[:, m:m + 1],
                        scalar1=alpha, scalar2=rowsum,
                        op0=ALU.mult, op1=ALU.add))
                    last_acc_upd = b.op("vector", lambda e, m=m: e.scalar_tensor_tensor(
                        out=acc[:, m, :], in0=acc[:, m, :], scalar=alpha,
                        in1=psV[:], op0=ALU.mult, op1=ALU.add), "sV")

            emit_A(0)
            nv = len(visits)
            for j in range(nv):
                if j + 1 < nv and (visits[j + 1][1] % 2) != (visits[j][1] % 2):
                    emit_A(j + 1)
                    emit_B(j)
                else:
                    emit_B(j)
                    if j + 1 < nv:
                        emit_A(j + 1)

            for m in range(8):
                k_tile = bat * 8 + m
                b.barrier("vector", ["sS"])
                b.barrier("scalar", ["sT"])
                b.wait("sync", "sV", xst2_free)
                b.dma("sync", xst2, x_in[k_tile], "qS")
                lv = hand()
                sv = ln_apply_recl(acc[:, m, :], aout, recl, lrun[:, m:m + 1])
                b.wait("vector", "sS", sv)
                b.wait("vector", "sQ", lv)
                b.wait("vector", "sQ", x1t_store)
                vv = b.op("vector", lambda e: e.tensor_tensor(
                    out=x1t, in0=xst2, in1=aout, op=ALU.add), "sV")
                xst2_free = vv
                b.wait("sync", "sV", vv)
                b.dma("sync", x1_d[k_tile], x1t, "qS")
                x1t_store = hand()
                b.wait("scalar", "sV", vv)
                sv = ln_apply(x1t, xlnb2)
                b.wait("vector", "sS", sv)
                b.op("vector", lambda e, k=k_tile: e.tensor_reduce(
                    out=maxcols[:, k:k + 1], in_=xlnb2, axis=AX.X,
                    op=ALU.max, apply_absolute_value=True), "sV")
                b.wait("tensor", "sS", sv)
                b.barrier("tensor", ["sV"])
                for c in range(8):
                    vt = b.op("tensor", lambda e, c=c: e.transpose(
                        out=psT[:, c * 128:(c + 1) * 128],
                        in_=xlnb2[:, c * 128:(c + 1) * 128], identity=identb),
                        "sT" if c == 7 else None)
                b.wait("vector", "sT", vt)
                b.wait("vector", "sQ", xlts2_store)
                cvv = b.op("vector", lambda e: e.tensor_copy(
                    out=xlts2.rearrange("p c t -> p (c t)"), in_=psT[:]), "sV")
                b.wait("sync", "sV", cvv)
                b.dma("sync", xlnT_d[:, :, k_tile * 128:(k_tile + 1) * 128].rearrange(
                    "c p t -> p c t"), xlts2, "qS")
                xlts2_store = hand()
                b.wait("tensor", "sV", cvv)

        b.barrier("vector", ["sV"])
        vv = b.op("vector", lambda e: e.tensor_reduce(
            out=cmax, in_=maxcols, axis=AX.X, op=ALU.max), "sV")
        b.wait("vector", "sV", vv)
        fold_max(cmax, 0)
        allreduce_scales(0, [(0, 0)])

        af.pop(); ab.pop(); ai.pop()

        af.push(); ab.push(); ai.push()
        b.barrier("sync", ["sT", "sV", "sS"])
        gmaxc = af.alloc((96,))
        umaxc = af.alloc((96,))
        gq = af.alloc((2048,))
        sg2 = af.alloc((2048,))
        xffn = af.alloc((D,))
        oout = af.alloc((D,))
        wT = ab.alloc((8, I3))
        xqT = ab.alloc((8, 2048))
        gstg = ab.alloc((4, 512))
        gchunk = ab.alloc((4, 2048))
        ocol = ab.alloc((2, 24, 128))
        i32t = ai.alloc((2048,))
        i32u = ai.alloc((2048,))

        b.dma("sync", xqT, xlnT_d.rearrange("c p t -> p c t"), "qS")
        lv = hand()
        b.wait("vector", "sQ", lv)
        for c in range(8):
            vv = b.op("vector", lambda e, c=c: e.tensor_scalar(
                out=i32t, in0=xqT[:, c, :], scalar1=rvec[0], scalar2=None, op0=ALU.mult), "sV")
            b.wait("vector", "sV", vv)
            vv = b.op("vector", lambda e, c=c: e.tensor_scalar(
                out=xqT[:, c, :], in0=i32t, scalar1=svec[0], scalar2=None, op0=ALU.mult),
                "sV")
            b.wait("vector", "sV", vv)
        xq_ready = b.cnt["sV"]

        b.wait("sync", "qW", wq_gather_done)

        def load_wT(widx):
            for R in range(8):
                for c in range(8):
                    b.dma("sync", wT[:, c, 384 * R:384 * (R + 1)],
                          wq_g[R, widx, c, :, :], "qS")
            return hand()

        sgq = af.alloc((2048,))

        chain1_state = {"p2_last": [0, 0], "p2r_last": [0, 0], "started": False,
                        "gate_store_done": 0, "ld_tick": [0, 0], "s": 0,
                        "i32t_free": 0, "u16_v": 0}

        def c1_load(i):
            st = chain1_state
            cb = i % 2
            b.wait("sync", "sQ", st["p2_last"][cb])
            b.wait("sync", "sV", st["p2r_last"][cb])
            if not st["started"]:
                b.wait("sync", "sQg", st["gate_store_done"])
                st["started"] = True
            b.dma("sync", gchunk[:, cb, :], gate_d[i], "qS")
            st["ld_tick"][cb] = hand()

        def c1_s1(j):
            st = chain1_state
            cb = j % 2
            b.wait("scalar", "sQ", st["ld_tick"][cb])
            b.wait("scalar", "sS", st["i32t_free"])
            b.barrier("scalar", ["sV"])
            st["s"] = b.op("scalar", lambda e, cb=cb: e.activation(
                out=i32t, in_=gchunk[:, cb, :], func=AF.Identity, scale=rvec[1]),
                "sS")

        def c1_s2(j):
            st = chain1_state
            b.wait("scalar", "sS", st["s"])
            st["s"] = st["i32t_free"] = b.op("scalar", lambda e: e.activation(
                out=gq, in_=i32t, func=AF.Identity, scale=svec[1]), "sS")

        def c1_s3(j):
            st = chain1_state
            b.wait("scalar", "sS", st["s"])
            st["s"] = b.op("scalar", lambda e: e.activation(
                out=sg2, in_=gq, func=AF.Sigmoid), "sS")

        def c1_s4v(j):
            st = chain1_state
            cb = j % 2
            b.wait("scalar", "sS", st["s"])
            b.wait("scalar", "sV", st["u16_v"])
            sv = b.op("scalar", lambda e: e.activation(
                out=i32u, in_=sg2, func=AF.Identity, scale=65536.0), "sS")
            b.wait("vector", "sS", sv)
            vv = b.op("vector", lambda e: e.tensor_scalar(
                out=sgq, in0=i32u, scalar1=65535.0, scalar2=1.0 / 65536.0,
                op0=ALU.min, op1=ALU.mult), "sV")
            st["u16_v"] = vv
            b.wait("vector", "sV", vv)
            vv = b.op("vector", lambda e, cb=cb: e.tensor_tensor(
                out=gchunk[:, cb, :], in0=gq, in1=sgq, op=ALU.mult), "sV")
            b.wait("vector", "sV", vv)
            st["p2r_last"][cb] = b.op("vector", lambda e, j=j, cb=cb: e.tensor_reduce(
                out=gmaxc[:, j:j + 1], in_=gchunk[:, cb, :], axis=AX.X,
                op=ALU.max, apply_absolute_value=True), "sV")
            b.wait("sync", "sV", vv)
            b.dma("sync", gate_d[j], gchunk[:, cb, :], "qS")
            st["p2_last"][cb] = hand()

        def chain1_pump(i, th):
            if th == 0:
                c1_load(i)
                if i >= 1:
                    c1_s1(i - 1)
            elif th == 1:
                if i >= 1:
                    c1_s2(i - 1)
            elif th == 2:
                if i >= 1:
                    c1_s3(i - 1)
            else:
                if i >= 1:
                    c1_s4v(i - 1)

        def mm_updown(dst_d, maxc, wready, interleave=None):
            b.wait("tensor", "sQ", wready)
            b.wait("tensor", "sV", xq_ready)
            b.barrier("tensor", ["sS", "sV", "sGp"])
            b.barrier("gpsimd", ["sS"])
            pfree2 = [0, 0, 0, 0]
            pe2 = [None] * 4
            gl = [0, 0, 0, 0]
            sj = 0
            gh_tick = [0, 0, 0, 0]
            for i in range(24):
                for th in range(4):
                    sl = sj % 4
                    if pe2[sl] is not None:
                        b.wait("tensor", pe2[sl], pfree2[sl])
                    for c in range(8):
                        mv = b.op("tensor", lambda e, i=i, th=th, c=c, sl=sl: e.matmul(
                            out=pslot[sl], lhsT=wT[:, c, i * 128:(i + 1) * 128],
                            rhs=xqT[:, c, th * 512:(th + 1) * 512],
                            start=(c == 0), stop=(c == 7)), "sT" if c == 7 else None)
                    gb = sj % 4
                    veng, vsem = ("scalar", "sS") if sj % 2 == 0 else ("vector", "sV")
                    b.wait(veng, "sT", mv)
                    b.wait(veng, "sQg", gh_tick[gb])
                    b.wait(veng, "sV", gl[gb])
                    if veng == "scalar":
                        ev = b.op("scalar", lambda e, sl=sl, gb=gb: e.activation(
                            out=gstg[:, gb, :], in_=pslot[sl], func=AF.Identity),
                            "sS")
                    else:
                        ev = b.op("vector", lambda e, sl=sl, gb=gb: e.tensor_copy(
                            out=gstg[:, gb, :], in_=pslot[sl]), "sV")
                    pfree2[sl], pe2[sl] = ev, vsem
                    b.wait("vector", vsem, ev)
                    gl[gb] = b.op("vector", lambda e, gb=gb, col=sj, maxc=maxc:
                                  e.tensor_reduce(
                        out=maxc[:, col:col + 1], in_=gstg[:, gb, :], axis=AX.X,
                        op=ALU.max, apply_absolute_value=True), "sV")
                    b.wait("gpsimd", vsem, ev)
                    b.dma("gpsimd", dst_d[i, :, th * 512:(th + 1) * 512],
                          gstg[:, gb, :], "qG")
                    if gb == 3:
                        hv = ghand()
                        gh_tick[0] = gh_tick[1] = gh_tick[2] = gh_tick[3] = hv
                    sj += 1
                    if interleave is not None:
                        interleave(i, th)
            return b.op("vector", lambda e: e.nop(), "sV")

        wready = load_wT(0)
        gdone = mm_updown(gate_d, gmaxc, wready)
        chain1_state["gate_store_done"] = b.cnt["sQg"]
        b.wait("vector", "sV", gdone)
        vv = b.op("vector", lambda e: e.tensor_reduce(
            out=gq[:, 0:1], in_=gmaxc, axis=AX.X, op=ALU.max), "sV")
        b.wait("vector", "sV", vv)
        fold_max(gq[:, 0:1], 1)
        allreduce_scales(1, [(1, 1)])

        b.barrier("sync", ["sT"])
        wready = load_wT(1)
        udone = mm_updown(up_d, umaxc, wready, interleave=chain1_pump)
        b.barrier("sync", ["sV", "sS", "sT"])
        c1_s1(23)
        c1_s2(23)
        c1_s3(23)
        c1_s4v(23)
        up_store_done = b.cnt["qS"]

        b.barrier("vector", ["sV", "sS"])
        vv = b.op("vector", lambda e: e.tensor_reduce(
            out=gq[:, 0:1], in_=umaxc, axis=AX.X, op=ALU.max), "sV")
        b.wait("vector", "sV", vv)
        fold_max(gq[:, 0:1], 2)
        vv = b.op("vector", lambda e: e.tensor_reduce(
            out=gq[:, 0:1], in_=gmaxc[:, 0:24], axis=AX.X, op=ALU.max), "sV")
        b.wait("vector", "sV", vv)
        fold_max(gq[:, 0:1], 3)
        allreduce_scales(2, [(2, 2), (3, 3)])

        wdT = wT.rearrange("p c o -> p (c o)")[:, 0:24 * 1024].rearrange(
            "p (i o) -> p i o", i=24)
        wd_view_g = [wq_g[R, 2].rearrange("c p o -> (c p o)").rearrange(
            "(i p o) -> i p o", i=24, p=128) for R in range(8)]

        s23 = af.alloc((1,))
        vv = b.op("vector", lambda e: e.tensor_scalar(
            out=s23, in0=svec[3], scalar1=svec[2], scalar2=None, op0=ALU.mult), "sV")
        b.wait("scalar", "sV", vv)
        b.wait("sync", "qS", up_store_done)
        upst_tick = hand()
        b.barrier("sync", ["sV", "sS"])
        b.wait("gpsimd", "sQ", upst_tick)
        b.barrier("gpsimd", ["sT", "sV", "sS"])
        gq2 = [gq, af.alloc((2048,))]
        Ubuf = [xqT[:, j, :] for j in range(4)]
        Pbuf = [xqT[:, 4 + j, :] for j in range(4)]
        Obuf = [gchunk[:, j, :] for j in range(4)]
        u_read = [0, 0, 0, 0]
        p_read = [0, 0, 0, 0]
        g_tt = [0, 0]
        o_store = [0, 0, 0, 0]
        utick = [0] * 24
        ptick = [0] * 24
        prev_tt = 0
        prev_s2 = 0
        p3r_last = 0

        pend_load = []
        pend_store = []

        def load_pair(j):
            sl = j % 4
            b.wait("gpsimd", "sV", u_read[sl])
            b.dma("gpsimd", Ubuf[sl], up_d[j], "qG")
            b.wait("gpsimd", "sS", p_read[sl])
            b.dma("gpsimd", Pbuf[sl], gate_d[j], "qG")
            pend_load.append(j)

        def tick_all():
            if pend_load:
                gt = ghand()
                for j in pend_load:
                    utick[j] = ptick[j] = gt
                pend_load.clear()
            if pend_store:
                pt = hand()
                for si in pend_store:
                    o_store[si % 4] = pt
                pend_store.clear()

        for j in range(4):
            load_pair(j)
        tick_all()
        wd_pairs = [(R, wi) for R in range(8) for wi in range(24)]
        for i in range(24):
            sl = i % 4
            cb = i % 2
            cbo = i % 4
            b.wait("scalar", "sQg", ptick[i])
            b.wait("scalar", "sS", prev_s2)
            s1 = b.op("scalar", lambda e, sl=sl: e.activation(
                out=i32t, in_=Pbuf[sl], func=AF.Identity, scale=rvec[3]), "sS")
            p_read[sl] = s1
            b.wait("scalar", "sS", s1)
            b.wait("scalar", "sV", g_tt[cb])
            prev_s2 = b.op("scalar", lambda e, cb=cb: e.activation(
                out=gq2[cb], in_=i32t, func=AF.Identity, scale=s23), "sS")
            b.wait("vector", "sQg", utick[i])
            b.wait("vector", "sV", prev_tt)
            v1 = b.op("vector", lambda e, sl=sl: e.tensor_scalar(
                out=i32u, in0=Ubuf[sl], scalar1=rvec[2], scalar2=None,
                op0=ALU.mult), "sV")
            u_read[sl] = v1
            b.wait("vector", "sV", v1)
            b.wait("vector", "sS", prev_s2)
            b.wait("vector", "sQ", o_store[cbo])
            tt = b.op("vector", lambda e, cb=cb, cbo=cbo: e.tensor_tensor(
                out=Obuf[cbo], in0=i32u, in1=gq2[cb], op=ALU.mult), "sV")
            g_tt[cb] = tt
            prev_tt = tt
            b.wait("vector", "sV", tt)
            p3r_last = b.op("vector", lambda e, i=i, cbo=cbo: e.tensor_reduce(
                out=umaxc[:, 24 + i:25 + i], in_=Obuf[cbo], axis=AX.X,
                op=ALU.max, apply_absolute_value=True), "sV")
            b.wait("sync", "sV", tt)
            b.dma("sync", up_d[i], Obuf[cbo], "qS")
            pend_store.append(i)
            if i + 4 < 24:
                load_pair(i + 4)
            if i == 0:
                b.barrier("sync", ["sT"])
            for Ri in wd_pairs[i * 8:(i + 1) * 8]:
                R, wi = Ri
                b.dma("sync", wdT[:, wi, R * 128:(R + 1) * 128],
                      wd_view_g[R][wi].rearrange("p o -> p o"), "qS")
            if i % 2 == 1:
                tick_all()
        tick_all()
        wd_ready = hand()

        b.wait("vector", "sV", p3r_last)
        vv = b.op("vector", lambda e: e.tensor_reduce(
            out=gq[:, 0:1], in_=umaxc[:, 24:48], axis=AX.X, op=ALU.max), "sV")
        b.wait("vector", "sV", vv)
        fold_max(gq[:, 0:1], 4)
        allreduce_scales(3, [(4, 4)])


        dslot = [psV[:, 0:512], psV[:, 512:1024], psA[:, 0:512], psA[:, 512:1024]]
        dfree = [0, 0, 0, 0]
        dj = 0
        ocol_last = [0, 0]
        oq_ready = [0, 0]
        out_st = 0
        b.barrier("tensor", ["sS", "sV", "sGp"])

        def emit_ocol(kt_i):
            ob = kt_i % 2
            b.wait("sync", "sT", ocol_last[ob])
            b.dma("sync", ocol[:, ob, :, :],
                  up_d[:, :, kt_i * 128:(kt_i + 1) * 128].rearrange(
                      "i p t -> p i t"), "qS")
            lv = hand()
            b.wait("vector", "sQ", lv)
            b.wait("vector", "sV", max(oq_ready[0], oq_ready[1]))
            vv = b.op("vector", lambda e, ob=ob: e.tensor_scalar(
                out=i32t, in0=ocol[:, ob, 0:16, :].rearrange("p i t -> p (i t)"),
                scalar1=rvec[4], scalar2=None, op0=ALU.mult), "sV")
            b.wait("vector", "sV", vv)
            vv = b.op("vector", lambda e, ob=ob: e.tensor_scalar(
                out=ocol[:, ob, 0:16, :].rearrange("p i t -> p (i t)"), in0=i32t,
                scalar1=svec[4], scalar2=None, op0=ALU.mult), "sV")
            b.wait("vector", "sV", vv)
            vv = b.op("vector", lambda e, ob=ob: e.tensor_scalar(
                out=i32t[:, 0:1024], in0=ocol[:, ob, 16:24, :].rearrange("p i t -> p (i t)"),
                scalar1=rvec[4], scalar2=None, op0=ALU.mult), "sV")
            b.wait("vector", "sV", vv)
            oq_ready[ob] = b.op("vector", lambda e, ob=ob: e.tensor_scalar(
                out=ocol[:, ob, 16:24, :].rearrange("p i t -> p (i t)"), in0=i32t[:, 0:1024],
                scalar1=svec[4], scalar2=None, op0=ALU.mult), "sV")

        b.barrier("vector", ["sS"])
        emit_ocol(0)
        for kt_i in range(NT):
            ob = kt_i % 2
            b.wait("tensor", "sV", oq_ready[ob])
            b.wait("tensor", "sQ", wd_ready)
            for h in range(2):
                sl = dj % 4
                b.wait("tensor", "sS", dfree[sl])
                for i in range(24):
                    mv = b.op("tensor", lambda e, h=h, i=i, sl=sl, ob=ob: e.matmul(
                        out=dslot[sl], lhsT=ocol[:, ob, i, :],
                        rhs=wdT[:, i, h * 512:(h + 1) * 512],
                        start=(i == 0), stop=(i == 23)), "sT" if i == 23 else None)
                b.wait("scalar", "sT", mv)
                ev = b.op("scalar", lambda e, sl=sl, h=h: e.activation(
                    out=oout[:, h * 512:(h + 1) * 512], in_=dslot[sl],
                    func=AF.Identity), "sS")
                dfree[sl] = ev
                dj += 1
            ocol_last[ob] = b.cnt["sT"]
            if kt_i + 1 < NT:
                emit_ocol(kt_i + 1)
            b.wait("sync", "sQ", out_st)
            b.dma("sync", xffn, x1_d[kt_i], "qS")
            lv = hand()
            b.wait("vector", "sS", dfree[(dj - 1) % 4])
            sv = ln_apply(oout, oout)
            b.wait("vector", "sS", sv)
            b.wait("vector", "sQ", lv)
            vv = b.op("vector", lambda e: e.tensor_tensor(
                out=xffn, in0=xffn, in1=oout, op=ALU.add), "sV")
            b.wait("sync", "sV", vv)
            b.dma("sync", out_d[kt_i], xffn, "qS")
            out_st = hand()
            b.wait("scalar", "sV", vv)

        b.wait("sync", "qS", b.cnt["qS"])
        af.pop(); ab.pop(); ai.pop()

        blk = es.enter_context(nc.Block())

        @blk.sync
        def _(e):
            for f in b.prog["sync"]:
                f(e)

        @blk.tensor
        def _(e):
            for f in b.prog["tensor"]:
                f(e)

        @blk.vector
        def _(e):
            for f in b.prog["vector"]:
                f(e)

        @blk.scalar
        def _(e):
            for f in b.prog["scalar"]:
                f(e)

        @blk.gpsimd
        def _(e):
            for f in b.prog["gpsimd"]:
                f(e)

    return nc


_NC_CACHE = None


def kernel(L0_attn_x, L0_attn_W, L0_attn_b, L0_ffn_w_gate, L0_ffn_w_up, L0_ffn_w_down):
    global _NC_CACHE
    x = np.ascontiguousarray(np.asarray(L0_attn_x, dtype=np.float32)).reshape(16384, D)
    W = np.ascontiguousarray(np.asarray(L0_attn_W, dtype=np.float32))
    bb = np.asarray(L0_attn_b, dtype=np.float32)
    wg = np.ascontiguousarray(np.asarray(L0_ffn_w_gate, dtype=np.float32))
    wu = np.ascontiguousarray(np.asarray(L0_ffn_w_up, dtype=np.float32))
    wd = np.ascontiguousarray(np.asarray(L0_ffn_w_down, dtype=np.float32))

    xt = x.reshape(128, 128, D)
    b_q = np.ascontiguousarray(bb[0:1024].reshape(8, 128).T)
    b_k = np.ascontiguousarray(bb[1024:2048].reshape(8, 128).T)
    b_v = np.ascontiguousarray(bb[2048:3072].reshape(1, 1024))
    identb = np.eye(128, dtype=ml_dtypes.bfloat16)
    identf = np.eye(128, dtype=np.float32)
    onesf = np.ones((1, 128), dtype=np.float32)

    in_maps = []
    for r in range(8):
        q = np.arange(128)[:, None]
        kk = np.arange(1024)[None, :]
        m = (kk <= 128 * r + q).astype(np.float32)
        mneg = (m - 1.0) * 1e30
        in_maps.append({
            "x_own": np.ascontiguousarray(xt[r::8]),
            "W": W,
            "b_q": b_q, "b_k": b_k, "b_v": b_v,
            "wg_sl": np.ascontiguousarray(
                wg[384 * r:384 * (r + 1)].reshape(3, 128, D).transpose(1, 0, 2)),
            "wu_sl": np.ascontiguousarray(
                wu[384 * r:384 * (r + 1)].reshape(3, 128, D).transpose(1, 0, 2)),
            "wd_sl": np.ascontiguousarray(wd[128 * r:128 * (r + 1)]),
            "mask1": m, "maskneg": mneg,
            "ident_bf": identb, "ident_f": identf, "ones_f": onesf,
        })
    if _NC_CACHE is None:
        _NC_CACHE = build()
    res = run_bass_kernel_spmd(_NC_CACHE, in_maps, core_ids=list(range(8)))
    global _LAST_RES
    _LAST_RES = res
    y = np.zeros((128, 128, D), dtype=np.float32)
    for r in range(8):
        y[r::8] = res.results[r]["out_own"]
    return y.reshape(2, 8192, D)
